# version 07
# baseline (speedup 1.0000x reference)
import numpy as np
import ml_dtypes
import concourse.bass as bass
import concourse.mybir as mybir
import concourse.bass_utils as bass_utils

F32 = mybir.dt.float32
BF16 = mybir.dt.bfloat16
Alu = mybir.AluOpType
Act = mybir.ActivationFunctionType
AX = mybir.AxisListType.X
bfloat16 = ml_dtypes.bfloat16

MAGIC = 1.5 * 2.0**23
DLY_STAT = 6000
DLY_RESC = 6000
DLY_GTOK = 6000
DLY_YADD = 0
NO_KTR = False
NO_SCAN = False
NO_INPLACE = False
P3_LVL = 2
QMAX = 32767.0
LN_EPS = 1e-5
DM = 1024
IM = 3072
HD = 64
CH = 64
FC = 8
IC = IM // 128


class SemC:
    def __init__(self, nc, name):
        self.h = nc.alloc_semaphore(name)
        self.v = 0


class _Cut(Exception):
    pass


def build_program(NC=8, T=2048, dbg=False, stop=None):
    try:
        return _build(NC, T, dbg, stop)
    except _Cut as c:
        return c.args[0]


def _build(NC=8, T=2048, dbg=False, stop=None):
    nc = bass.Bass("TRN2", target_bir_lowering=False, debug=False,
                   detect_race_conditions=False)
    TC = T // 128
    NTG = max(T // 512, 1)
    TGW = min(T, 512)
    NCH = T // CH
    USH = DM // NC
    UIC = USH // 128
    DSH = IM // NC
    DIC = DSH // 128

    def din(name, shape, dt=F32):
        return nc.dram_tensor(name, shape, dt, kind="ExternalInput").ap()

    x_in = din("x_in", [T, DM])
    wq_t = din("wq_t", [DM, DM])
    wk_t = din("wk_t", [DM, DM])
    wv_t = din("wv_t", [DM, DM])
    wg_t = din("wg_t", [DM, DM])
    wo_t = din("wo_t", [DM, DM])
    gamma_t = din("gamma_t", [128, FC])
    beta_t = din("beta_t", [128, FC])
    wup_s = din("wup_s", [IM, USH])
    wgt_s = din("wgt_s", [IM, USH])
    wdn_s = din("wdn_s", [DM, DSH])
    wup_ts = din("wup_ts", [USH, IM])
    wgt_ts = din("wgt_ts", [USH, IM])
    wdn_ts = din("wdn_ts", [DSH, DM])
    eye_f = din("eye_f", [128, 128])
    sel4 = din("sel4", [4, 128])
    tril = din("tril", [128, FC * CH], BF16)
    scanm = din("scanm", [128, T])
    onesch = din("onesch", [128, NCH])
    ones_f_1x = din("ones_f_1x", [1, 128])
    ones_bf_1x = din("ones_bf_1x", [1, 128], BF16)
    ones_bf_c1 = din("ones_bf_c1", [128, 1], BF16)
    selH = din("selH", [128, NC])

    out_d = nc.dram_tensor("out", [T, DM], F32, kind="ExternalOutput").ap()

    def dint(name, shape, dt):
        return nc.dram_tensor(name, shape, dt, kind="Internal").ap()

    spill_up = dint("spill_up", [IM, T], BF16)
    spill_gt = dint("spill_gt", [IM, T], BF16)
    spill_g2 = dint("spill_g2", [IM, T], BF16)
    spill_v = dint("spill_v", [T, DM], BF16)
    agup_i = dint("agup_i", [USH, IM], BF16)
    agup_o = dint("agup_o", [DM, IM], BF16)
    aggt_i = dint("aggt_i", [USH, IM], BF16)
    aggt_o = dint("aggt_o", [DM, IM], BF16)
    agdn_i = dint("agdn_i", [DSH, DM], BF16)
    agdn_o = dint("agdn_o", [IM, DM], BF16)
    aga_i = dint("aga_i", [1, 1], F32)
    aga_o = dint("aga_o", [NC, 1], F32)
    agh_i = dint("agh_i", [129, 512], F32)
    agh_o = dint("agh_o", [129 * NC, 512], F32)
    agc_i = dint("agc_i", [1, 1], F32)
    agc_o = dint("agc_o", [NC, 1], F32)
    agd_i = dint("agd_i", [1, 1], F32)
    agd_o = dint("agd_o", [NC, 1], F32)
    dbg_t = {}
    if dbg:
        for nm, shape, dt in [("dbg_xq", [FC * 128, T], BF16),
                              ("dbg_qf", [FC * 128, T], BF16),
                              ("dbg_kf", [FC * 128, T], BF16),
                              ("dbg_v", [T, DM], BF16),
                              ("dbg_y", [FC * 128, T], BF16),
                              ("dbg_yl", [FC * 128, T], BF16),
                              ("dbg_s", [128, TC, DM], BF16),
                              ("dbg_g", [128, TC, DM], BF16),
                              ("dbg_o", [IC * 128, T], BF16),
                              ("dbg_up", [IM, T], BF16),
                              ("dbg_gt", [IM, T], BF16),
                              ("dbg_g2", [IM, T], BF16),
                              ("dbg_sr", [128, 12], F32),
                              ("dbg_wup", [DM, IM], BF16),
                              ("dbg_wdn", [IM, DM], BF16),
                              ("dbg_h", [128, FC * HD], F32),
                              ("dbg_y7", [FC * 128, T], BF16),
                              ("dbg_hin", [128, FC * HD], F32),
                              ("dbg_ec", [128, FC * NCH], F32),
                              ("dbg_e2", [128, FC * NCH], F32)]:
            dbg_t[nm] = nc.dram_tensor(nm, shape, dt,
                                       kind="ExternalOutput").ap()

    AR = 212800
    arena = nc.alloc_sbuf_tensor("arena", [128, AR], mybir.dt.uint8)
    base = nc.lookup_mloc(arena).addr
    cur = [0]

    def sb(name, shape, dt, at=None):
        nbytes = int(np.prod(shape[1:])) * mybir.dt.size(dt)
        if at is None:
            at = cur[0]
            cur[0] = at + ((nbytes + 31) & ~31)
        assert at + nbytes <= AR, f"{name}: arena overflow {at + nbytes}"
        return nc.alloc_sbuf_tensor_at(name, shape, dt, offset=base + at)

    eye_f_sb = sb("eye_f_sb", [128, 128], F32)
    sel4_sb = sb("sel4_sb", [4, 128], F32)
    tril_sb = sb("tril_sb", [128, FC * CH], BF16)
    onesch_sb = sb("onesch_sb", [128, NCH], F32)
    of1x_sb = sb("of1x_sb", [1, 128], F32)
    obf1x_sb = sb("obf1x_sb", [1, 128], BF16)
    obfc1_sb = sb("obfc1_sb", [128, 1], BF16)
    gamma_sb = sb("gamma_sb", [128, FC], F32)
    beta_sb = sb("beta_sb", [128, FC], F32)
    beta_bf = sb("beta_bf", [128, FC], BF16)
    selH_sb = sb("selH_sb", [128, NC], F32)
    lcc_sb = sb("lcc_sb", [128, FC, NCH], F32)
    ec_sb = sb("ec_sb", [128, FC, NCH], F32)
    macc_sb = sb("macc_sb", [128, 8], F32)
    m2_sb = sb("m2_sb", [2, 1], F32)
    mfin_sb = sb("mfin_sb", [1, 64], F32)
    srA_sb = sb("srA_sb", [128, 2], F32)
    srB_sb = sb("srB_sb", [128, 4], F32)
    srC_sb = sb("srC_sb", [128, 2], F32)
    srD_sb = sb("srD_sb", [128, 2], F32)
    stat_sb = sb("stat_sb", [128, 16], F32)
    SALL_N = max((IM // 128) * (USH // 32), (DM // 128) * (DSH // 32))
    sall_sb = sb("sall_sb", [128, SALL_N], F32)
    rall_sb = sb("rall_sb", [128, SALL_N], F32)

    atA = cur[0]; xn = sb("xn", [128, FC, T], BF16)
    atB = cur[0]; qf = sb("qf", [128, FC, T], BF16)
    atC = cur[0]; kf = sb("kf", [128, FC, T], BF16)
    atE = cur[0]; ktok = sb("ktok", [128, TC, DM], BF16)
    stok = sb("stok", [128, TC, DM], BF16, at=atE)
    atF = cur[0]; scanm_sb = sb("scanm_sb", [128, T], F32)
    yf = sb("yf", [128, FC, T], BF16, at=atF)
    gtok = sb("gtok", [128, TC, DM], BF16, at=atF)
    cur[0] = atF + max(T * 4, FC * T * 2, TC * DM * 2)
    o_all = sb("o_all", [128, IC, T], BF16, at=atA)
    assert IC * T * 2 <= (atE - atA), "o_all overlay too big"

    atS = cur[0]
    cur[0] += 40 * 1024
    assert cur[0] <= AR, f"arena used {cur[0]}"

    def scr(name, shape, dt, off):
        nbytes = int(np.prod(shape[1:])) * mybir.dt.size(dt)
        assert off + nbytes <= 40 * 1024, f"{name} scratch overflow"
        return nc.alloc_sbuf_tensor_at(name, shape, dt, offset=base + atS + off)

    wdq = scr("wdq", [128, IM], F32, 0)
    sT_sc = scr("sT_sc", [4, IM], F32, 12 * 1024)
    rT_sc = scr("rT_sc", [4, IM], F32, 24 * 1024)
    wqb = scr("wqb", [128, 2, 512], BF16, 36 * 1024)
    fdq = scr("fdq", [128, 512], F32, 38 * 1024)
    xld = scr("xld", [128, 2, DM], F32, 0)
    xnt = scr("xnt", [128, 2, DM], BF16, 8 * 1024)
    sqs = scr("sqs", [128, DM], F32, 12 * 1024)
    wbuf = scr("wbuf", [128, FC, DM], BF16, 0)
    prep_a = scr("prep_a", [128, T], F32, 16 * 1024)
    prep_c = scr("prep_c", [128, T], F32, 24 * 1024)
    hld_sb = scr("hld_sb", [128, 2, 512], F32, 0)
    zrow_sb = scr("zrow_sb", [1, 512], F32, 0)
    vbuf_sb = scr("vbuf_sb", [128, 2, DM], BF16, 4 * 1024)
    h_sb = scr("h_sb", [128, FC, HD], F32, 8 * 1024)
    sm_sb = scr("sm_sb", [128, 2, 512], BF16, 10 * 1024)
    g0_sb = scr("g0_sb", [128, FC, HD], BF16, 12 * 1024)
    qhat_sb = scr("qhat_sb", [128, FC, HD], BF16, 13 * 1024)
    hin_bf = scr("hin_bf", [128, FC, HD], BF16, 14 * 1024)
    hin_sb = scr("hin_sb", [128, FC, HD], F32, 15 * 1024)
    e2_sb = scr("e2_sb", [128, FC, NCH], F32, 17 * 1024)
    bo_sb = scr("bo_sb", [1, DM], BF16, 18 * 1024)
    fxq = scr("fxq", [128, T], F32, 32 * 1024)
    vstg = scr("vstg", [128, 2, DM], BF16, 32 * 1024)
    ugw = scr("ugw", [128, 2, FC, 128], BF16, 0)
    ustg = scr("ustg", [128, 2, T], BF16, 4 * 1024)
    fscA = scr("fscA", [128, T], F32, 4 * 1024)
    fscB = scr("fscB", [128, T], F32, 12 * 1024)
    gq_bf = scr("gq_bf", [128, T], BF16, 20 * 1024)
    g2st = scr("g2st", [128, 2, T], BF16, 24 * 1024)
    grl = scr("grl", [128, 2, T], BF16, 32 * 1024)
    wo_bf = scr("wo_bf", [128, FC, DM], BF16, 0)
    y2s = scr("y2s", [128, FC, 512], BF16, 16 * 1024)
    mu_g = scr("mu_g", [1, 512], F32, 24 * 1024)
    rl_g = scr("rl_g", [1, 512], F32, 26 * 1024)
    vv_g = scr("vv_g", [1, 512], F32, 28 * 1024)
    nw_g = scr("nw_g", [1, 512], F32, 30 * 1024)
    url = scr("url", [128, 2, T], BF16, 0)
    g2rl = scr("g2rl", [128, 2, T], BF16, 8 * 1024)
    foq = scr("foq", [128, T], F32, 16 * 1024)
    uq_bf = scr("uq_bf", [128, T], BF16, 24 * 1024)
    g2q_bf = scr("g2q_bf", [128, T], BF16, 28 * 1024)
    dnw = scr("dnw", [128, 2, DM], BF16, 32 * 1024)
    xrl = scr("xrl", [128, 2, DM], F32, 0)
    sgt = scr("sgt", [128, DM], BF16, 8 * 1024)
    ost = scr("ost", [128, 2, DM], F32, 12 * 1024)

    ps = nc.alloc_psum_tensor("ps", [128, 8, 512], F32)

    def bank(i):
        return ps[:, i, :]

    qs = SemC(nc, "qs")
    qc = SemC(nc, "qc")
    qv = SemC(nc, "qv")
    qg = SemC(nc, "qg")
    cc = SemC(nc, "cc")
    te = SemC(nc, "te")
    ve = SemC(nc, "ve")
    se = SemC(nc, "se")
    barS = SemC(nc, "barS")

    blk_ctx = nc.Block()
    blk = blk_ctx.__enter__()

    def barrier():
        barS.v += 5
        tgt = barS.v

        def mk(e):
            e.drain()
            e.nop().then_inc(barS.h, 1)
            e.wait_ge(barS.h, tgt)
        blk.tensor(mk)
        blk.vector(mk)
        blk.scalar(mk)
        blk.gpsimd(mk)
        blk.sync(mk)

    RG = [list(range(NC))]

    def cut(n):
        if stop is not None and stop == n:
            def drain_s(e):
                e.wait_ge(qs.h, qs.v)
            def drain_c(e):
                e.wait_ge(qc.h, qc.v)
            def drain_g(e):
                e.wait_ge(qg.h, qg.v)
                e.wait_ge(cc.h, cc.v)
            blk.sync(drain_s)
            blk.scalar(drain_c)
            blk.gpsimd(drain_g)
            barrier()
            blk_ctx.__exit__(None, None, None)
            raise _Cut(nc)

    def p0_loads(e):
        for dst, src in [(eye_f_sb, eye_f), (sel4_sb, sel4), (tril_sb, tril),
                         (onesch_sb, onesch), (of1x_sb, ones_f_1x),
                         (obf1x_sb, ones_bf_1x), (obfc1_sb, ones_bf_c1),
                         (gamma_sb, gamma_t), (beta_sb, beta_t),
                         (selH_sb, selH), (scanm_sb, scanm)]:
            e.dma_start(out=dst[:], in_=src).then_inc(qs.h, 16)
            qs.v += 16
    blk.sync(p0_loads)
    CONST_DONE = qs.v

    prev_dq_done = [0]

    def dequant2(st_d, tr_d, ag_in, O, SHW, nic):
        noc = O // 128
        nblk = SHW // 32
        sW = sall_sb[:, 0:noc * nblk].rearrange("p (a b) -> p a b", b=nblk)
        rW = rall_sb[:, 0:noc * nblk].rearrange("p (a b) -> p a b", b=nblk)
        ld0 = qs.v
        vb0 = ve.v
        PREV = prev_dq_done[0]

        def ld(e):
            e.wait_ge(qs.h, CONST_DONE)
            e.wait_ge(ve.h, PREV)
            for oc in range(noc):
                if oc >= 1:
                    e.wait_ge(ve.h, vb0 + oc)
                e.dma_start(out=wdq[:, 0:SHW],
                            in_=st_d[oc * 128:(oc + 1) * 128, :]).then_inc(qs.h, 16)
                qs.v += 16
        blk.sync(ld)

        def vmax(e):
            for oc in range(noc):
                e.wait_ge(qs.h, ld0 + (oc + 1) * 16)
                src3 = wdq[:, 0:SHW].rearrange("p (b w) -> p b w", w=32)
                e.reduce_max(sW[:, oc, :], src3, axis=AX,
                             apply_absolute_value=True)
                e.drain()
                e.tensor_scalar(sW[:, oc, :], sW[:, oc, :], 1.0 / 7.0, 1e-12,
                                Alu.mult, Alu.max)
                e.drain()
                e.reciprocal(rW[:, oc, :], sW[:, oc, :]).then_inc(ve.h, 1)
                ve.v += 1
        blk.vector(vmax)
        SC_DONE = ve.v

        for icx in range(nic):
            tb0 = te.v
            TCP_PREV = ve.v

            def ttr(e, icx=icx, tb0=tb0, TCP_PREV=TCP_PREV):
                e.wait_ge(ve.h, SC_DONE)
                if icx >= 1:
                    e.wait_ge(ve.h, TCP_PREV)
                for oc in range(noc):
                    if oc >= 1:
                        e.wait_ge(ve.h, TCP_PREV + oc)
                    e.transpose(ps[0:4, 4, 0:128].bitcast(F32),
                                sW[:, oc, 4 * icx:4 * icx + 4], eye_f_sb[:])
                    e.transpose(ps[0:4, 5, 0:128].bitcast(F32),
                                rW[:, oc, 4 * icx:4 * icx + 4],
                                eye_f_sb[:]).then_inc(te.h, 1)
                    te.v += 1
            blk.tensor(ttr)

            def tcp(e, icx=icx, tb0=tb0):
                for oc in range(noc):
                    e.wait_ge(te.h, tb0 + oc + 1)
                    e.tensor_copy(sT_sc[0:4, oc * 128:(oc + 1) * 128],
                                  ps[0:4, 4, 0:128])
                    e.tensor_copy(rT_sc[0:4, oc * 128:(oc + 1) * 128],
                                  ps[0:4, 5, 0:128]).then_inc(ve.h, 1)
                    ve.v += 1
            blk.vector(tcp)
            TCP_DONE = ve.v

            ld2b = qs.v

            def ld2(e, icx=icx, TCP_DONE=TCP_DONE):
                e.wait_ge(ve.h, TCP_DONE)
                e.dma_start(out=wdq[:, 0:O],
                            in_=tr_d[icx * 128:(icx + 1) * 128, 0:O]).then_inc(qs.h, 16)
                qs.v += 16
            blk.sync(ld2)
            LD2_DONE = qs.v

            nog = O // 512
            mm0 = te.v
            DQ0 = ve.v
            DQG0 = qg.v

            def selmm(e, mm0=mm0, DQ0=DQ0, nog=nog, TCP_DONE=TCP_DONE):
                e.wait_ge(ve.h, TCP_DONE)
                for og in range(nog):
                    if og >= 1:
                        e.wait_ge(ve.h, DQ0 + og)
                    e.matmul(bank(6), sel4_sb[:],
                             sT_sc[0:4, og * 512:(og + 1) * 512],
                             start=True, stop=True)
                    e.matmul(bank(7), sel4_sb[:],
                             rT_sc[0:4, og * 512:(og + 1) * 512],
                             start=True, stop=True).then_inc(te.h, 1)
                    te.v += 1
            blk.tensor(selmm)

            def dq(e, icx=icx, mm0=mm0, LD2_DONE=LD2_DONE, DQG0=DQG0, nog=nog):
                e.wait_ge(qs.h, LD2_DONE)
                for og in range(nog):
                    e.wait_ge(te.h, mm0 + og + 1)
                    if og >= 2:
                        e.wait_ge(qg.h, DQG0 + (og - 1) * 16)
                    w = wdq[:, og * 512:(og + 1) * 512]
                    o = wqb[:, og % 2, :]
                    e.tensor_tensor(fdq[:], w, bank(7), Alu.mult)
                    e.tensor_scalar(fdq[:], fdq[:], MAGIC, None, Alu.add)
                    e.scalar_tensor_tensor(o, fdq[:], -MAGIC, bank(6),
                                           Alu.add, Alu.mult).then_inc(ve.h, 1)
                    ve.v += 1
            blk.vector(dq)
            DQ_DONE = ve.v

            def st(e, icx=icx, DQ0=DQ0, nog=nog):
                for og in range(nog):
                    e.wait_ge(ve.h, DQ0 + og + 1)
                    e.dma_start(out=ag_in[icx * 128:(icx + 1) * 128,
                                          og * 512:(og + 1) * 512],
                                in_=wqb[:, og % 2, :]).then_inc(qg.h, 16)
                    qg.v += 16
            blk.gpsimd(st)
            prev_dq_done[0] = DQ_DONE
        return qg.v

    dequant2(wup_s, wup_ts, agup_i, IM, USH, UIC)
    dequant2(wgt_s, wgt_ts, aggt_i, IM, USH, UIC)
    DQ_ALL = dequant2(wdn_s, wdn_ts, agdn_i, DM, DSH, DIC)

    def p0_ag(e):
        e.wait_ge(qg.h, DQ_ALL)
        e.collective_compute("AllGather", Alu.bypass, replica_groups=RG,
                             ins=[agup_i.opt()], outs=[agup_o.opt()]).then_inc(cc.h, 1)
        e.collective_compute("AllGather", Alu.bypass, replica_groups=RG,
                             ins=[aggt_i.opt()], outs=[aggt_o.opt()]).then_inc(cc.h, 1)
        e.collective_compute("AllGather", Alu.bypass, replica_groups=RG,
                             ins=[agdn_i.opt()], outs=[agdn_o.opt()]).then_inc(cc.h, 1)
        cc.v += 3
    blk.gpsimd(p0_ag)
    CC_W = cc.v

    barrier()
    cut(1)

    P1_LD0 = qs.v
    se1 = se.v
    ve1 = ve.v
    qc1 = qc.v

    def p1_load(e):
        e.wait_ge(qs.h, CONST_DONE)
        for t in range(TC):
            if t >= 2:
                e.wait_ge(ve.h, ve1 + (t - 2) * 3 + 3)
            e.dma_start(out=xld[:, t % 2, :],
                        in_=x_in[t * 128:(t + 1) * 128, :]).then_inc(qs.h, 16)
            qs.v += 16
    blk.sync(p1_load)

    def p1_scalar(e):
        for t in range(TC):
            e.wait_ge(qs.h, P1_LD0 + (t + 1) * 16)
            e.square(sqs[:], xld[:, t % 2, :]).then_inc(se.h, 1)
            se.v += 1
            e.wait_ge(ve.h, ve1 + t * 3 + 2)
            e.sqrt(stat_sb[:, 3:4], stat_sb[:, 1:2]).then_inc(se.h, 1)
            se.v += 1
            e.wait_ge(ve.h, ve1 + t * 3 + 3)
            for j in range(FC):
                e.dma_start_transpose(
                    xn[:, j, t * 128:(t + 1) * 128],
                    xnt[:, t % 2, j * 128:(j + 1) * 128]).then_inc(qc.h, 16)
            qc.v += FC * 16
    blk.scalar(p1_scalar)
    XN_T_DONE = qc.v

    def p1_vector(e):
        for t in range(TC):
            e.wait_ge(qs.h, P1_LD0 + (t + 1) * 16)
            x_c = xld[:, t % 2, :]
            e.reduce_sum(stat_sb[:, 0:1], x_c, axis=AX)
            e.drain()
            e.tensor_scalar(stat_sb[:, 0:1], stat_sb[:, 0:1], 1.0 / DM, None,
                            Alu.mult).then_inc(ve.h, 1)
            ve.v += 1
            e.wait_ge(se.h, se1 + t * 2 + 1)
            e.reduce_sum(stat_sb[:, 1:2], sqs[:], axis=AX)
            e.drain()
            e.tensor_scalar(stat_sb[:, 1:2], stat_sb[:, 1:2], 1.0 / DM, None,
                            Alu.mult)
            e.tensor_tensor(stat_sb[:, 2:3], stat_sb[:, 0:1], stat_sb[:, 0:1],
                            Alu.mult)
            e.drain()
            e.tensor_tensor(stat_sb[:, 1:2], stat_sb[:, 1:2], stat_sb[:, 2:3],
                            Alu.subtract)
            e.drain()
            e.tensor_scalar(stat_sb[:, 1:2], stat_sb[:, 1:2], LN_EPS, None,
                            Alu.add).then_inc(ve.h, 1)
            ve.v += 1
            e.wait_ge(se.h, se1 + t * 2 + 2)
            e.reciprocal(stat_sb[:, 4:5], stat_sb[:, 3:4])
            e.drain()
            e.tensor_tensor(stat_sb[:, 5:6], stat_sb[:, 4:5], stat_sb[:, 4:5],
                            Alu.mult)
            e.drain()
            e.tensor_tensor(stat_sb[:, 5:6], stat_sb[:, 5:6], stat_sb[:, 1:2],
                            Alu.mult)
            e.drain()
            e.tensor_scalar(stat_sb[:, 5:6], stat_sb[:, 5:6], -0.5, 1.5,
                            Alu.mult, Alu.add)
            e.drain()
            e.tensor_tensor(stat_sb[:, 4:5], stat_sb[:, 4:5], stat_sb[:, 5:6],
                            Alu.mult)
            e.drain()
            if t >= 2:
                e.wait_ge(qc.h, qc1 + (t - 1) * FC * 16)
            e.tensor_scalar(xnt[:, t % 2, :], x_c, stat_sb[:, 0:1],
                            stat_sb[:, 4:5], Alu.subtract, Alu.mult)
            e.reduce_max(stat_sb[:, 6:7], xnt[:, t % 2, :], axis=AX,
                         apply_absolute_value=True)
            e.drain()
            if t == 0:
                e.tensor_copy(macc_sb[:, 0:1], stat_sb[:, 6:7]).then_inc(ve.h, 1)
            else:
                e.tensor_tensor(macc_sb[:, 0:1], macc_sb[:, 0:1],
                                stat_sb[:, 6:7], Alu.max).then_inc(ve.h, 1)
            ve.v += 1
    blk.vector(p1_vector)
    VE_P1 = ve.v

    def p1_af_t(e):
        e.wait_ge(ve.h, VE_P1)
        e.transpose(ps[0:1, 7, 0:128].bitcast(F32), macc_sb[:, 0:1],
                    eye_f_sb[:]).then_inc(te.h, 1)
        te.v += 1
    blk.tensor(p1_af_t)
    A_TR = te.v

    def p1_af_v(e):
        e.wait_ge(te.h, A_TR)
        e.reduce_max(mfin_sb[0:1, 0:1], ps[0:1, 7, 0:128], axis=AX).then_inc(ve.h, 1)
        ve.v += 1
    blk.vector(p1_af_v)
    A_SC = ve.v

    def p1_ag(e):
        e.wait_ge(ve.h, A_SC)
        e.dma_start(out=aga_i, in_=mfin_sb[0:1, 0:1]).then_inc(qg.h, 16)
        qg.v += 16
        e.wait_ge(qg.h, qg.v)
        e.collective_compute("AllGather", Alu.bypass, replica_groups=RG,
                             ins=[aga_i.opt()], outs=[aga_o.opt()]).then_inc(cc.h, 1)
        cc.v += 1
        e.wait_ge(cc.h, cc.v)
        e.dma_start(out=mfin_sb[0:1, 8:8 + NC], in_=aga_o.opt()).then_inc(qg.h, 16)
        qg.v += 16
    blk.gpsimd(p1_ag)
    CC_A = cc.v
    AGA_LD = qg.v

    def p2_v(e):
        e.wait_ge(qg.h, AGA_LD)
        e.reduce_max(mfin_sb[0:1, 0:1], mfin_sb[0:1, 8:8 + NC], axis=AX)
        e.drain()
        e.tensor_scalar(mfin_sb[0:1, 1:2], mfin_sb[0:1, 0:1], 1.0 / QMAX,
                        1e-12, Alu.mult, Alu.max)
        e.drain()
        e.reciprocal(mfin_sb[0:1, 2:3], mfin_sb[0:1, 1:2]).then_inc(ve.h, 1)
        ve.v += 1
    blk.vector(p2_v)
    P2_V = ve.v

    def p2_t(e):
        e.wait_ge(ve.h, P2_V)
        e.matmul(ps[:, 7, 0:2].bitcast(F32), of1x_sb[:], mfin_sb[0:1, 1:3],
                 start=True, stop=True).then_inc(te.h, 1)
        te.v += 1
    blk.tensor(p2_t)
    P2_T = te.v

    def p2_v2(e):
        e.wait_ge(te.h, P2_T)
        e.tensor_copy(srA_sb[:], ps[:, 7, 0:2]).then_inc(ve.h, 1)
        ve.v += 1
    blk.vector(p2_v2)

    barrier()
    cut(2)

    projw = [wq_t, wk_t, wv_t, wg_t]
    qg3 = qg.v
    te3 = te.v
    NQK = 2 * FC * NTG
    NV = TC * 2
    te_q_done = te3 + FC * NTG
    te_k_done = te3 + NQK
    te_v_done = te3 + NQK + NV
    te_g_done = te3 + NQK + NV + FC * NTG
    SPJ = NTG + 3

    def p3_wload(e):
        e.wait_ge(qc.h, XN_T_DONE)
        e.wait_ge(ve.h, VE_P1)
        for i in range(4):
            if i == 1:
                e.wait_ge(te.h, te_q_done)
            if i == 2:
                e.wait_ge(te.h, te_k_done)
            if i == 3:
                e.wait_ge(te.h, te_v_done)
            w = projw[i]
            for j in range(FC):
                e.dma_start(out=wbuf[:, j, :],
                            in_=w[j * 128:(j + 1) * 128, :]).then_inc(qg.h, 16)
            qg.v += FC * 16
    blk.gpsimd(p3_wload)

    ve3 = ve.v
    se3 = se.v

    def p3_mm(e):
        e.wait_ge(qc.h, XN_T_DONE)
        k = 0
        for i in range(2):
            e.wait_ge(qg.h, qg3 + (i + 1) * FC * 16)
            for j in range(FC):
                for g in range(NTG):
                    k += 1
                    b = (k - 1) % 4
                    if k > 4:
                        e.wait_ge(ve.h, ve3 + (k - 4))
                    for jin in range(FC):
                        ins = e.matmul(bank(b)[:, 0:TGW],
                                       wbuf[:, jin, j * 128:(j + 1) * 128],
                                       xn[:, jin, g * TGW:(g + 1) * TGW],
                                       start=(jin == 0), stop=(jin == FC - 1))
                    ins.then_inc(te.h, 1)
                    te.v += 1
        e.wait_ge(qg.h, qg3 + 3 * FC * 16)
        for t in range(TC):
            for g in range(2):
                k += 1
                b = (k - 1) % 4
                e.wait_ge(ve.h, ve3 + (k - 4))
                for jin in range(FC):
                    ins = e.matmul(bank(b)[:],
                                   xn[:, jin, t * 128:(t + 1) * 128],
                                   wbuf[:, jin, g * 512:(g + 1) * 512],
                                   start=(jin == 0), stop=(jin == FC - 1))
                ins.then_inc(te.h, 1)
                te.v += 1
        if P3_LVL < 1:
            return
        e.wait_ge(qg.h, qg3 + 4 * FC * 16)
        for j in range(FC):
            for g in range(NTG):
                k += 1
                b = (k - 1) % 4
                kg = j * NTG + g
                if kg >= 4:
                    jj, gg = divmod(kg - 4, NTG)
                    e.wait_ge(se.h, se3 + jj * SPJ + gg + 1)
                else:
                    e.wait_ge(ve.h, ve3 + NQK + NV)
                for jin in range(FC):
                    ins = e.matmul(bank(b)[:, 0:TGW],
                                   wbuf[:, jin, j * 128:(j + 1) * 128],
                                   xn[:, jin, g * TGW:(g + 1) * TGW],
                                   start=(jin == 0), stop=(jin == FC - 1))
                ins.then_inc(te.h, 1)
                te.v += 1
    blk.tensor(p3_mm)

    VSP0 = qs.v

    def p3_ev(e):
        k = 0
        for i in range(2):
            dst = qf if i == 0 else kf
            for j in range(FC):
                for g in range(NTG):
                    k += 1
                    e.wait_ge(te.h, te3 + k)
                    e.tensor_copy(dst[:, j, g * TGW:(g + 1) * TGW],
                                  bank((k - 1) % 4)[:, 0:TGW]).then_inc(ve.h, 1)
                    ve.v += 1
        for t in range(TC):
            for g in range(2):
                k += 1
                e.wait_ge(te.h, te3 + k)
                if t >= 2 and g == 0:
                    e.wait_ge(qs.h, VSP0 + (t - 1) * 16)
                e.tensor_copy(vstg[:, t % 2, g * 512:(g + 1) * 512],
                              bank((k - 1) % 4)[:]).then_inc(ve.h, 1)
                ve.v += 1
    blk.vector(p3_ev)
    VE_EV = ve.v

    VSP0 = qs.v

    def p3_vspill(e):
        for t in range(TC):
            e.wait_ge(ve.h, ve3 + NQK + (t + 1) * 2)
            e.dma_start(out=spill_v[t * 128:(t + 1) * 128, :],
                        in_=vstg[:, t % 2, :]).then_inc(qs.h, 16)
            qs.v += 16
    blk.sync(p3_vspill)
    V_SPILL = qs.v

    ve_prep0 = ve.v

    def p3_gs(e):
        if P3_LVL < 1:
            return
        for j in range(FC):
            if j >= 1:
                e.wait_ge(ve.h, ve_prep0 + j * 4)
            for g in range(NTG):
                kk = NQK + NV + j * NTG + g + 1
                e.wait_ge(te.h, te3 + kk)
                e.activation(prep_a[:, g * TGW:(g + 1) * TGW],
                             bank((kk - 1) % 4)[:, 0:TGW],
                             Act.Sigmoid).then_inc(se.h, 1)
                se.v += 1
            if NO_INPLACE:
                e.activation(fscA[:], prep_a[:], Act.Ln)
                e.activation(prep_a[:], fscA[:], Act.Copy).then_inc(se.h, 1)
            else:
                e.activation(prep_a[:], prep_a[:], Act.Ln).then_inc(se.h, 1)
            se.v += 1
            e.wait_ge(ve.h, ve_prep0 + j * 4 + 2)
            if NO_INPLACE:
                e.activation(fscA[:], prep_a[:], Act.Exp)
                e.activation(prep_a[:], fscA[:], Act.Copy).then_inc(se.h, 1)
            else:
                e.activation(prep_a[:], prep_a[:], Act.Exp).then_inc(se.h, 1)
            se.v += 1
            e.activation(ec_sb[:, j, :], lcc_sb[:, j, :],
                         Act.Exp).then_inc(se.h, 1)
            se.v += 1
    blk.scalar(p3_gs)

    def p3_prep_v(e):
        if P3_LVL < 1:
            return
        for j in range(FC):
            e.wait_ge(se.h, se3 + j * SPJ + NTG + 1)
            if NO_SCAN:
                e.tensor_copy(prep_c[:], prep_a[:])
            else:
                e.tensor_tensor_scan(prep_c[:], scanm_sb[:], prep_a[:], 0.0,
                                     Alu.mult, Alu.add)
            e.drain()
            e.tensor_copy(lcc_sb[:, j, :], prep_c[:, CH - 1::CH]).then_inc(ve.h, 1)
            ve.v += 1
            for c2 in range(NCH):
                ins = e.tensor_scalar(prep_a[:, c2 * CH:(c2 + 1) * CH],
                                      prep_c[:, c2 * CH:(c2 + 1) * CH],
                                      prep_c[:, (c2 + 1) * CH - 1:
                                              (c2 + 1) * CH],
                                      None, Alu.subtract)
            ins.then_inc(ve.h, 1)
            ve.v += 1
            e.wait_ge(se.h, se3 + j * SPJ + NTG + 2)
            e.tensor_tensor(qf[:, j, :], qf[:, j, :], prep_a[:],
                            Alu.mult).then_inc(ve.h, 1)
            ve.v += 1
            e.reciprocal(prep_a[:], prep_a[:])
            e.tensor_tensor(kf[:, j, :], kf[:, j, :], prep_a[:],
                            Alu.mult).then_inc(ve.h, 1)
            ve.v += 1
    blk.vector(p3_prep_v)
    VE_PREP = ve.v

    qs3 = qs.v

    def p3_ktr(e):
        if NO_KTR or P3_LVL < 2:
            return
        for j in range(FC):
            e.wait_ge(ve.h, ve_prep0 + (j + 1) * 4)
            for t in range(TC):
                e.dma_start_transpose(
                    ktok[:, t, j * 128:(j + 1) * 128],
                    kf[:, j, t * 128:(t + 1) * 128]).then_inc(qs.h, 16)
            qs.v += TC * 16
    blk.sync(p3_ktr)
    KTOK_DONE = qs.v

    def p3_xq(e):
        e.wait_ge(te.h, te_g_done if P3_LVL >= 1 else te_v_done)
        e.wait_ge(qs.h, V_SPILL)
        for j in range(FC):
            e.tensor_scalar(fxq[:], xn[:, j, :], srA_sb[:, 1:2], MAGIC,
                            Alu.mult, Alu.add)
            e.tensor_scalar(xn[:, j, :], fxq[:], -MAGIC, srA_sb[:, 0:1],
                            Alu.add, Alu.mult).then_inc(ve.h, 1)
            ve.v += 1
    blk.vector(p3_xq)

    barrier()
    if dbg and "d3" in dbg:
        def d3(e):
            for j in range(FC):
                e.dma_start(out=dbg_t["dbg_xq"][j * 128:(j + 1) * 128, :],
                            in_=xn[:, j, :]).then_inc(qs.h, 16)
                e.dma_start(out=dbg_t["dbg_qf"][j * 128:(j + 1) * 128, :],
                            in_=qf[:, j, :]).then_inc(qs.h, 16)
                e.dma_start(out=dbg_t["dbg_kf"][j * 128:(j + 1) * 128, :],
                            in_=kf[:, j, :]).then_inc(qs.h, 16)
                qs.v += 48
            e.dma_start(out=dbg_t["dbg_ec"],
                        in_=ec_sb[:].rearrange("p f w -> p (f w)")).then_inc(qs.h, 16)
            qs.v += 16
            e.wait_ge(qs.h, qs.v)
        blk.sync(d3)
        barrier()
    cut(3)

    qs5 = qs.v
    te5 = te.v
    se5 = se.v
    ve5 = ve.v
    qv5 = qv.v

    def p5_wload(e):
        e.wait_ge(cc.h, CC_W)
        for w in range(2):
            src = agup_o if w == 0 else aggt_o
            for icx in range(IC):
                kk = w * IC + icx
                if kk >= 2:
                    e.wait_ge(te.h, te5 + (kk - 1) * NTG)
                for jin in range(FC):
                    e.dma_start(out=ugw[:, kk % 2, jin, :],
                                in_=src[jin * 128:(jin + 1) * 128,
                                        icx * 128:(icx + 1) * 128]).then_inc(qs.h, 16)
                qs.v += FC * 16
    blk.sync(p5_wload)

    def p5_mm(e):
        for kk in range(2 * IC):
            e.wait_ge(qs.h, qs5 + (kk + 1) * FC * 16)
            if kk >= 1:
                e.wait_ge(se.h, se5 + kk * NTG)
            for jin in range(FC):
                for g in range(NTG):
                    ins = e.matmul(bank(g % 4)[:, 0:TGW],
                                   ugw[:, kk % 2, jin, :],
                                   xn[:, jin, g * TGW:(g + 1) * TGW],
                                   start=(jin == 0), stop=(jin == FC - 1))
                    if jin == FC - 1:
                        ins.then_inc(te.h, 1)
                        te.v += 1
    blk.tensor(p5_mm)

    qc5 = qc.v

    def p5_evac(e):
        for kk in range(2 * IC):
            if kk >= 2:
                e.wait_ge(qc.h, qc5 + (kk - 1) * 16)
                e.wait_ge(ve.h, ve5 + (kk - 2) + 1)
            for g in range(NTG):
                e.wait_ge(te.h, te5 + kk * NTG + g + 1)
                e.activation(ustg[:, kk % 2, g * TGW:(g + 1) * TGW],
                             bank(g % 4)[:, 0:TGW], Act.Copy).then_inc(se.h, 1)
                se.v += 1
            dst = spill_up if kk < IC else spill_gt
            icx = kk % IC
            e.wait_ge(se.h, se5 + (kk + 1) * NTG)
            e.dma_start(out=dst[icx * 128:(icx + 1) * 128, :],
                        in_=ustg[:, kk % 2, :]).then_inc(qc.h, 16)
            qc.v += 16
    blk.scalar(p5_evac)

    def p5_vec(e):
        for kk in range(2 * IC):
            e.wait_ge(se.h, se5 + (kk + 1) * NTG)
            col = 1 if kk < IC else 2
            e.reduce_max(stat_sb[:, 8:9], ustg[:, kk % 2, :], axis=AX,
                         apply_absolute_value=True)
            e.drain()
            if kk % IC == 0:
                e.tensor_copy(macc_sb[:, col:col + 1],
                              stat_sb[:, 8:9]).then_inc(ve.h, 1)
            else:
                e.tensor_tensor(macc_sb[:, col:col + 1],
                                macc_sb[:, col:col + 1], stat_sb[:, 8:9],
                                Alu.max).then_inc(ve.h, 1)
            ve.v += 1
    blk.vector(p5_vec)
    UG_SPILL = qc.v
    VE_UGMAX = ve.v

    def p5_bt(e):
        e.wait_ge(ve.h, VE_UGMAX)
        e.transpose(ps[0:2, 7, 0:128].bitcast(F32), macc_sb[:, 1:3],
                    eye_f_sb[:]).then_inc(te.h, 1)
        te.v += 1
    blk.tensor(p5_bt)
    B_TR = te.v

    def p5_bv(e):
        e.wait_ge(te.h, B_TR)
        e.reduce_max(m2_sb[0:2, 0:1], ps[0:2, 7, 0:128], axis=AX).then_inc(ve.h, 1)
        ve.v += 1
    blk.vector(p5_bv)
    B_SC = ve.v

    barrier()
    cut(4)

    te6 = te.v
    ve6 = ve.v
    qc6 = qc.v

    def p6_vload(e):
        e.wait_ge(qs.h, V_SPILL)
        for c in range(NCH):
            if c >= 2:
                e.wait_ge(te.h, te6 + (c - 2) * 3 + 3)
            sl2 = c % 2
            e.dma_start(out=vbuf_sb[0:64, sl2, :],
                        in_=spill_v[c * 64:(c + 1) * 64, :]).then_inc(qc.h, 16)
            e.dma_start(out=vbuf_sb[64:128, sl2, :],
                        in_=spill_v[c * 64:(c + 1) * 64, :]).then_inc(qc.h, 16)
            qc.v += 32
    blk.scalar(p6_vload)

    def p6_hinit(e):
        e.wait_ge(qc.h, UG_SPILL)
        e.memset(zrow_sb[:], 0.0)
        e.memset(h_sb[:], 0.0).then_inc(ve.h, 1)
        ve.v += 1
    blk.vector(p6_hinit)
    ve6b = ve.v

    def sbanks(par):
        return (4, 5) if par == 0 else (1, 3)

    def p6_mm(e):
        e.wait_ge(qs.h, KTOK_DONE)
        e.wait_ge(ve.h, ve6b)
        for c in range(NCH):
            par = c % 2
            e.wait_ge(qc.h, qc6 + (c + 1) * 32)
            if c >= 2:
                e.wait_ge(ve.h, ve6b + (c - 2) * 4 + 4)
            sb0, sb1 = sbanks(par)
            for p in range(FC):
                for hh in range(2):
                    rb = 64 * hh
                    sp = bank(sb0 if hh == 0 else sb1)
                    ins = e.matmul(
                        sp[rb:rb + 64, p * 64:(p + 1) * 64],
                        kf[rb:rb + 64, p, c * 64:(c + 1) * 64],
                        qf[rb:rb + 64, p, c * 64:(c + 1) * 64],
                        start=True, stop=True, tile_position=(rb, rb),
                        skip_group_check=True)
            ins.then_inc(te.h, 1)
            te.v += 1
            kvb = bank(6 if par == 0 else 7)
            for p in range(FC):
                for hh in range(2):
                    ins = e.matmul(
                        kvb[hh * 64:(hh + 1) * 64, p * 64:(p + 1) * 64],
                        ktok[par * 64:(par + 1) * 64, c // 2,
                             (2 * p + hh) * 64:(2 * p + hh + 1) * 64],
                        vbuf_sb[par * 64:(par + 1) * 64, par,
                                (2 * p + hh) * 64:(2 * p + hh + 1) * 64],
                        start=True, stop=True, tile_position=(par * 64, hh * 64),
                        skip_group_check=True)
            ins.then_inc(te.h, 1)
            te.v += 1
            e.wait_ge(ve.h, ve6b + c * 4 + 2)
            yb = bank(0 if par == 0 else 2)
            for p in range(FC):
                for hh in range(2):
                    rb = 64 * hh
                    e.matmul(yb[rb:rb + 64, p * 64:(p + 1) * 64],
                             vbuf_sb[rb:rb + 64, par,
                                     (2 * p + hh) * 64:(2 * p + hh + 1) * 64],
                             sm_sb[rb:rb + 64, hh, p * 64:(p + 1) * 64],
                             start=True, stop=False,
                             tile_position=(rb, rb),
                             skip_group_check=True)
                    ins = e.matmul(
                        yb[rb:rb + 64, p * 64:(p + 1) * 64],
                        g0_sb[rb:rb + 64, p, :],
                        qf[rb:rb + 64, p, c * 64:(c + 1) * 64],
                        start=False, stop=True, tile_position=(rb, rb),
                        skip_group_check=True)
            ins.then_inc(te.h, 1)
            te.v += 1
    blk.tensor(p6_mm)
    TE_LOOP_END = te.v

    def p6_v(e):
        for c in range(NCH):
            par = c % 2
            sb0, sb1 = sbanks(par)
            e.wait_ge(te.h, te6 + c * 3 + 1)
            for hh in range(2):
                rb = 64 * hh
                sp = bank(sb0 if hh == 0 else sb1)
                ins = e.tensor_tensor(
                    sm_sb[rb:rb + 64, hh, :],
                    sp[rb:rb + 64, :],
                    tril_sb[rb:rb + 64, :], Alu.mult)
            ins.then_inc(ve.h, 1)
            ve.v += 1
            for f2 in range(FC):
                ins = e.tensor_scalar(g0_sb[:, f2, :], h_sb[:, f2, :],
                                      ec_sb[:, f2, c:c + 1], None, Alu.mult)
            ins.then_inc(ve.h, 1)
            ve.v += 1
            if c >= 1:
                e.wait_ge(te.h, te6 + (c - 1) * 3 + 3)
                pprev = (c - 1) % 2
                e.tensor_copy(yf[:, :, (c - 1) * 64:c * 64],
                              bank(0 if pprev == 0 else 2).rearrange(
                                  "p (f w) -> p f w", w=64)).then_inc(ve.h, 1)
            else:
                e.tensor_copy(stat_sb[:, 11:12],
                              stat_sb[:, 8:9]).then_inc(ve.h, 1)
            ve.v += 1
            e.wait_ge(te.h, te6 + c * 3 + 2)
            e.drain()
            kvb = bank(6 if par == 0 else 7)
            e.tensor_tensor(h_sb[:], g0_sb[:],
                            kvb.rearrange("p (f w) -> p f w", w=HD),
                            Alu.add).then_inc(ve.h, 1)
            ve.v += 1
        e.wait_ge(te.h, te6 + (NCH - 1) * 3 + 3)
        pprev = (NCH - 1) % 2
        e.tensor_copy(yf[:, :, (NCH - 1) * 64:NCH * 64],
                      bank(0 if pprev == 0 else 2).rearrange(
                          "p (f w) -> p f w", w=64)).then_inc(ve.h, 1)
        ve.v += 1
    blk.vector(p6_v)
    VE_LOOP_END = ve.v

    def p6_hag(e):
        e.wait_ge(ve.h, VE_LOOP_END)
        e.dma_start(out=agh_i[0:128, :],
                    in_=h_sb[:].rearrange("p f w -> p (f w)")).then_inc(qg.h, 16)
        qg.v += 16
        e.dma_start(out=agh_i[128:129, :], in_=zrow_sb[:]).then_inc(qg.h, 16)
        qg.v += 16
        e.wait_ge(qg.h, qg.v)
        e.dma_start(out=agh_i[128:129, 0:2],
                    in_=m2_sb[0:2, 0:1]).then_inc(qg.h, 16)
        qg.v += 16
        e.wait_ge(qg.h, qg.v)
        e.collective_compute("AllGather", Alu.bypass, replica_groups=RG,
                             ins=[agh_i.opt()], outs=[agh_o.opt()]).then_inc(cc.h, 1)
        cc.v += 1
    blk.gpsimd(p6_hag)
    CC_H = cc.v

    barrier()
    if dbg and "d6" in dbg:
        def d6(e):
            for j in range(FC):
                e.dma_start(out=dbg_t["dbg_y"][j * 128:(j + 1) * 128, :],
                            in_=yf[:, j, :]).then_inc(qs.h, 16)
                qs.v += 16
            e.dma_start(out=dbg_t["dbg_h"],
                        in_=h_sb[:].rearrange("p f w -> p (f w)")).then_inc(qs.h, 16)
            qs.v += 16
            e.wait_ge(qs.h, qs.v)
        blk.sync(d6)
        barrier()

    qs7 = qs.v
    ve7 = ve.v
    te7 = te.v
    se7 = se.v
    B_FIN = ve7 + 1
    SRB_CP = ve7 + 2
    HACC0 = ve7 + 2
    HIN_BF = ve7 + 2 + NC + 1
    E2V0 = HIN_BF
    E2V_END = E2V0 + FC
    CORR0 = E2V_END
    GP0 = CORR0 + 2 * NCH
    GP_END = GP0 + 3 * IC
    C_SC = GP_END + 1
    C_FIN = C_SC + 1
    SRC_DONE = C_FIN + 1
    B_BC = te7 + 1
    CORR_T0 = te7 + 1
    C_TR = te7 + 1 + NCH + 1
    C_BC = C_TR + 1
    E2_S = se7 + 1
    G2GRL0 = qs.v + 16 + NC * 16

    def p7_sync(e):
        e.wait_ge(cc.h, CC_H)
        mm = agh_o[128::129, 0:2].transpose([1, 0])
        e.dma_start(out=mfin_sb[0:1, 16:16 + 2 * NC].rearrange(
            "p (a b) -> p a b", b=NC), in_=mm,
            allow_slow_non_contiguous=True).then_inc(qs.h, 16)
        qs.v += 16
        for r in range(NC):
            if r >= 2:
                e.wait_ge(ve.h, HACC0 + r - 1)
            e.dma_start(out=hld_sb[:, r % 2, :],
                        in_=agh_o[r * 129:r * 129 + 128, :]).then_inc(qs.h, 16)
            qs.v += 16
        e.wait_ge(qc.h, UG_SPILL)
        for icx in range(IC):
            if icx >= 2:
                e.wait_ge(ve.h, GP0 + (icx - 2) * 3 + 3)
            e.dma_start(out=grl[:, icx % 2, :],
                        in_=spill_gt[icx * 128:(icx + 1) * 128, :]).then_inc(qs.h, 16)
            qs.v += 16
            if icx >= 2:
                e.wait_ge(ve.h, GP0 + (icx - 2) * 3 + 2)
                e.dma_start(out=spill_g2[(icx - 2) * 128:(icx - 1) * 128, :],
                            in_=g2st[:, icx % 2, :]).then_inc(qs.h, 16)
                qs.v += 16
        for icx in range(IC - 2, IC):
            e.wait_ge(ve.h, GP0 + icx * 3 + 2)
            e.dma_start(out=spill_g2[icx * 128:(icx + 1) * 128, :],
                        in_=g2st[:, icx % 2, :]).then_inc(qs.h, 16)
            qs.v += 16
    blk.sync(p7_sync)
    G2SP_END = qs.v

    def p7_scalar(e):
        e.wait_ge(ve.h, E2V_END)
        e.activation(e2_sb[:].rearrange("p f w -> p (f w)"),
                     e2_sb[:].rearrange("p f w -> p (f w)"),
                     Act.Exp).then_inc(se.h, 1)
        se.v += 1
        for icx in range(IC):
            e.wait_ge(ve.h, GP0 + icx * 3 + 1)
            e.activation(fscB[:], gq_bf[:], Act.Sigmoid)
            e.activation(fscB[:], fscB[:], Act.Copy, bias=MAGIC,
                         scale=65536.0).then_inc(se.h, 1)
            se.v += 1
    blk.scalar(p7_scalar)

    def p7_tensor(e):
        e.wait_ge(ve.h, B_FIN)
        e.matmul(ps[:, 7, 0:4].bitcast(F32), of1x_sb[:], mfin_sb[0:1, 10:14],
                 start=True, stop=True).then_inc(te.h, 1)
        te.v += 1
        for c in range(NCH):
            e.wait_ge(ve.h, CORR0 + c * 2 + 1)
            if c >= 2:
                e.wait_ge(ve.h, CORR0 + (c - 2) * 2 + 2)
            cb = bank(c % 2)
            for p in range(FC):
                for hh in range(2):
                    rb = hh * 64
                    e.matmul(cb[rb:rb + 64, p * 64:(p + 1) * 64],
                             hin_bf[rb:rb + 64, p, :],
                             qhat_sb[rb:rb + 64, p, :],
                             start=True, stop=True,
                             tile_position=(rb, rb),
                             skip_group_check=True)
            e.drain()
            e.nop().then_inc(te.h, 1)
            te.v += 1
        e.wait_ge(ve.h, GP_END)
        e.transpose(ps[0:1, 6, 0:128].bitcast(F32), macc_sb[:, 3:4],
                    eye_f_sb[:]).then_inc(te.h, 1)
        te.v += 1
        e.wait_ge(ve.h, C_FIN)
        e.matmul(ps[:, 7, 0:2].bitcast(F32), of1x_sb[:], mfin_sb[0:1, 21:23],
                 start=True, stop=True).then_inc(te.h, 1)
        te.v += 1
    blk.tensor(p7_tensor)

    def p7_vector(e):
        e.wait_ge(qs.h, qs7 + 16)
        e.memset(hin_sb[:], 0.0)
        e.drain()
        e.reduce_max(mfin_sb[0:1, 4:6],
                     mfin_sb[0:1, 16:16 + 2 * NC].rearrange(
                         "p (a b) -> p a b", b=NC), axis=AX)
        e.drain()
        e.tensor_scalar(mfin_sb[0:1, 6:8], mfin_sb[0:1, 4:6], 1.0 / QMAX,
                        1e-12, Alu.mult, Alu.max)
        e.drain()
        e.reciprocal(mfin_sb[0:1, 8:10], mfin_sb[0:1, 6:8])
        e.drain()
        e.tensor_copy(mfin_sb[0:1, 10:11], mfin_sb[0:1, 6:7])
        e.tensor_copy(mfin_sb[0:1, 11:12], mfin_sb[0:1, 8:9])
        e.tensor_copy(mfin_sb[0:1, 12:13], mfin_sb[0:1, 7:8])
        e.tensor_copy(mfin_sb[0:1, 13:14], mfin_sb[0:1, 9:10]).then_inc(ve.h, 1)
        ve.v += 1
        e.wait_ge(te.h, B_BC)
        e.tensor_copy(srB_sb[:], ps[:, 7, 0:4]).then_inc(ve.h, 1)
        ve.v += 1
        for r in range(NC):
            e.wait_ge(qs.h, qs7 + 16 + (r + 1) * 16)
            e.scalar_tensor_tensor(hin_sb[:].rearrange("p f w -> p (f w)"),
                                   hld_sb[:, r % 2, :], selH_sb[:, r:r + 1],
                                   hin_sb[:].rearrange("p f w -> p (f w)"),
                                   Alu.mult, Alu.add).then_inc(ve.h, 1)
            ve.v += 1
        e.drain()
        e.tensor_copy(hin_bf[:], hin_sb[:]).then_inc(ve.h, 1)
        ve.v += 1
        for j in range(FC):
            e.tensor_tensor_scan(e2_sb[:, j, :], onesch_sb[:], lcc_sb[:, j, :],
                                 0.0, Alu.mult, Alu.add).then_inc(ve.h, 1)
            ve.v += 1
        e.wait_ge(se.h, E2_S)
        for c in range(NCH):
            if c >= 1:
                e.wait_ge(te.h, CORR_T0 + c)
            for f2 in range(FC):
                ins = e.tensor_scalar(qhat_sb[:, f2, :],
                                      qf[:, f2, c * 64:(c + 1) * 64],
                                      e2_sb[:, f2, c:c + 1], None, Alu.mult)
            ins.then_inc(ve.h, 1)
            ve.v += 1
            e.wait_ge(te.h, CORR_T0 + c + 1)
            if DLY_YADD:
                e.nop(cycle_cnt=DLY_YADD)
            e.tensor_tensor(yf[:, :, c * 64:(c + 1) * 64],
                            yf[:, :, c * 64:(c + 1) * 64],
                            bank(c % 2).rearrange("p (f w) -> p f w", w=64),
                            Alu.add).then_inc(ve.h, 1)
            ve.v += 1
        for icx in range(IC):
            e.wait_ge(qs.h, qs7 + 16 + NC * 16 + (icx + 1) * 16)
            e.tensor_scalar(fscA[:], grl[:, icx % 2, :], srB_sb[:, 3:4],
                            MAGIC, Alu.mult, Alu.add)
            e.tensor_scalar(gq_bf[:], fscA[:], -MAGIC, srB_sb[:, 2:3],
                            Alu.add, Alu.mult).then_inc(ve.h, 1)
            ve.v += 1
            e.wait_ge(se.h, E2_S + icx + 1)
            if icx >= 2:
                e.wait_ge(qs.h, G2GRL0 + icx * 16 + (icx - 1) * 16)
            e.tensor_scalar(fscB[:], fscB[:], -MAGIC, 65535.0, Alu.add,
                            Alu.min)
            e.tensor_tensor(g2st[:, icx % 2, :], gq_bf[:], fscB[:],
                            Alu.mult).then_inc(ve.h, 1)
            ve.v += 1
            e.reduce_max(stat_sb[:, 9:10], g2st[:, icx % 2, :], axis=AX,
                         apply_absolute_value=True)
            e.drain()
            if icx == 0:
                e.tensor_copy(macc_sb[:, 3:4],
                              stat_sb[:, 9:10]).then_inc(ve.h, 1)
            else:
                e.tensor_tensor(macc_sb[:, 3:4], macc_sb[:, 3:4],
                                stat_sb[:, 9:10], Alu.max).then_inc(ve.h, 1)
            ve.v += 1
    blk.vector(p7_vector)
    assert ve.v == GP_END, (ve.v, GP_END)
    assert te.v == C_BC, (te.v, C_BC)

    def p7_cv(e):
        e.wait_ge(te.h, C_TR)
        e.reduce_max(mfin_sb[0:1, 20:21], ps[0:1, 6, 0:128],
                     axis=AX).then_inc(ve.h, 1)
        ve.v += 1
    blk.vector(p7_cv)

    def p7_cag(e):
        e.wait_ge(ve.h, C_SC)
        e.dma_start(out=agc_i, in_=mfin_sb[0:1, 20:21]).then_inc(qg.h, 16)
        qg.v += 16
        e.wait_ge(qg.h, qg.v)
        e.collective_compute("AllGather", Alu.bypass, replica_groups=RG,
                             ins=[agc_i.opt()], outs=[agc_o.opt()]).then_inc(cc.h, 1)
        cc.v += 1
        e.wait_ge(cc.h, cc.v)
        e.dma_start(out=mfin_sb[0:1, 24:24 + NC], in_=agc_o.opt()).then_inc(qg.h, 16)
        qg.v += 16
    blk.gpsimd(p7_cag)
    CC_C = cc.v
    AGC_LD = qg.v

    def p7_cfin(e):
        e.wait_ge(qg.h, AGC_LD)
        e.reduce_max(mfin_sb[0:1, 20:21], mfin_sb[0:1, 24:24 + NC], axis=AX)
        e.drain()
        e.tensor_scalar(mfin_sb[0:1, 21:22], mfin_sb[0:1, 20:21],
                        1.0 / (65536.0 * QMAX), 1e-12, Alu.mult, Alu.max)
        e.drain()
        e.reciprocal(mfin_sb[0:1, 22:23], mfin_sb[0:1, 21:22])
        e.drain()
        e.tensor_scalar(mfin_sb[0:1, 22:23], mfin_sb[0:1, 22:23],
                        1.0 / 65536.0, None, Alu.mult).then_inc(ve.h, 1)
        ve.v += 1
    blk.vector(p7_cfin)

    def p7_cbv(e):
        e.wait_ge(te.h, C_BC)
        e.tensor_copy(srC_sb[:], ps[:, 7, 0:2]).then_inc(ve.h, 1)
        ve.v += 1
    blk.vector(p7_cbv)
    assert ve.v == SRC_DONE, (ve.v, SRC_DONE)

    barrier()
    if dbg and "d7y" in dbg:
        def d7y(e):
            for j in range(FC):
                e.dma_start(out=dbg_t["dbg_y7"][j * 128:(j + 1) * 128, :],
                            in_=yf[:, j, :]).then_inc(qs.h, 16)
                qs.v += 16
            e.wait_ge(qs.h, qs.v)
        blk.sync(d7y)
        barrier()
    cut(5)
    if dbg and "d7w" in dbg:
        def d7w(e):
            e.wait_ge(cc.h, CC_W)
            e.dma_start(out=dbg_t["dbg_wup"], in_=agup_o).then_inc(qc.h, 16)
            e.dma_start(out=dbg_t["dbg_wdn"], in_=agdn_o).then_inc(qc.h, 16)
            e.dma_start(out=dbg_t["dbg_up"], in_=spill_up).then_inc(qc.h, 16)
            e.dma_start(out=dbg_t["dbg_gt"], in_=spill_gt).then_inc(qc.h, 16)
            e.dma_start(out=dbg_t["dbg_g2"], in_=spill_g2).then_inc(qc.h, 16)
            qc.v += 80
            e.wait_ge(qc.h, qc.v)
        blk.scalar(d7w)

        def d7(e):
            e.dma_start(out=dbg_t["dbg_sr"][:, 0:2], in_=srA_sb[:]).then_inc(qs.h, 16)
            e.dma_start(out=dbg_t["dbg_sr"][:, 2:6], in_=srB_sb[:]).then_inc(qs.h, 16)
            e.dma_start(out=dbg_t["dbg_sr"][:, 6:8], in_=srC_sb[:]).then_inc(qs.h, 16)
            qs.v += 48
            e.wait_ge(qs.h, qs.v)
        blk.sync(d7)
        barrier()

    qg8 = qg.v

    def p8_wo(e):
        for j in range(FC):
            e.dma_start(out=wo_bf[:, j, :],
                        in_=wo_t[j * 128:(j + 1) * 128, :]).then_inc(qg.h, 16)
        qg.v += FC * 16
    blk.gpsimd(p8_wo)
    WO_LD = qg.v

    se8 = se.v
    te8 = te.v
    ve8 = ve.v

    def p8_scalar(e):
        for g in range(NTG):
            if g >= 1:
                e.wait_ge(te.h, te8 + 2 * g - 1)
            e.activation(y2s[:], yf[:, :, g * TGW:(g + 1) * TGW],
                         Act.Square).then_inc(se.h, 1)
            se.v += 1
            e.wait_ge(ve.h, ve8 + g * 3 + 1)
            e.sqrt(nw_g[0:1, 0:TGW], vv_g[0:1, 0:TGW]).then_inc(se.h, 1)
            se.v += 1
    blk.scalar(p8_scalar)

    def p8_stat_t(e):
        for g in range(NTG):
            e.wait_ge(se.h, se8 + g * 2 + 1)
            if g >= 2:
                e.wait_ge(ve.h, ve8 + (g - 2) * 3 + 1)
            for j in range(FC):
                e.matmul(ps[0:1, 4 + g % 2, 0:TGW], obfc1_sb[:],
                         yf[:, j, g * TGW:(g + 1) * TGW],
                         start=(j == 0), stop=(j == FC - 1),
                         skip_group_check=True)
                ins = e.matmul(ps[0:1, 6 + g % 2, 0:TGW], obfc1_sb[:],
                               y2s[:, j, 0:TGW],
                               start=(j == 0), stop=(j == FC - 1),
                               skip_group_check=True)
                if j == FC - 1:
                    e.drain()
                    e.nop().then_inc(te.h, 1)
                    te.v += 1
            e.wait_ge(ve.h, ve8 + g * 3 + 2)
            if g >= 1:
                e.wait_ge(ve.h, ve8 + (g - 1) * 3 + 3)
            e.matmul(ps[:, 2, 0:TGW].bitcast(F32), of1x_sb[:],
                     rl_g[0:1, 0:TGW], start=True, stop=True,
                     skip_group_check=True)
            e.matmul(ps[:, 3, 0:TGW].bitcast(F32), of1x_sb[:],
                     mu_g[0:1, 0:TGW], start=True, stop=True,
                     skip_group_check=True)
            e.drain()
            e.nop().then_inc(te.h, 1)
            te.v += 1
    blk.tensor(p8_stat_t)

    def p8_vector(e):
        for g in range(NTG):
            e.wait_ge(te.h, te8 + 2 * g + 1)
            if DLY_STAT:
                e.nop(cycle_cnt=DLY_STAT)
            e.tensor_scalar(mu_g[0:1, 0:TGW], ps[0:1, 4 + g % 2, 0:TGW],
                            1.0 / DM, None, Alu.mult)
            e.tensor_scalar(vv_g[0:1, 0:TGW], ps[0:1, 6 + g % 2, 0:TGW],
                            1.0 / DM, None, Alu.mult)
            e.drain()
            e.tensor_tensor(rl_g[0:1, 0:TGW], mu_g[0:1, 0:TGW],
                            mu_g[0:1, 0:TGW], Alu.mult)
            e.drain()
            e.tensor_tensor(vv_g[0:1, 0:TGW], vv_g[0:1, 0:TGW],
                            rl_g[0:1, 0:TGW], Alu.subtract)
            e.drain()
            e.tensor_scalar(vv_g[0:1, 0:TGW], vv_g[0:1, 0:TGW], LN_EPS, None,
                            Alu.add).then_inc(ve.h, 1)
            ve.v += 1
            e.wait_ge(se.h, se8 + g * 2 + 2)
            e.reciprocal(rl_g[0:1, 0:TGW], nw_g[0:1, 0:TGW])
            e.drain()
            e.tensor_tensor(nw_g[0:1, 0:TGW], rl_g[0:1, 0:TGW],
                            rl_g[0:1, 0:TGW], Alu.mult)
            e.drain()
            e.tensor_tensor(nw_g[0:1, 0:TGW], nw_g[0:1, 0:TGW],
                            vv_g[0:1, 0:TGW], Alu.mult)
            e.drain()
            e.tensor_scalar(nw_g[0:1, 0:TGW], nw_g[0:1, 0:TGW], -0.5, 1.5,
                            Alu.mult, Alu.add)
            e.drain()
            e.tensor_tensor(rl_g[0:1, 0:TGW], rl_g[0:1, 0:TGW],
                            nw_g[0:1, 0:TGW], Alu.mult)
            e.drain()
            e.tensor_tensor(mu_g[0:1, 0:TGW], mu_g[0:1, 0:TGW],
                            rl_g[0:1, 0:TGW], Alu.mult).then_inc(ve.h, 1)
            ve.v += 1
            e.wait_ge(te.h, te8 + 2 * g + 2)
            if DLY_RESC:
                e.nop(cycle_cnt=DLY_RESC)
            for j2 in range(FC):
                y2d = yf[:, j2, g * TGW:(g + 1) * TGW]
                e.tensor_tensor(y2d, y2d, ps[:, 2, 0:TGW], Alu.mult)
                ins = e.tensor_tensor(y2d, y2d, ps[:, 3, 0:TGW], Alu.subtract)
            ins.then_inc(ve.h, 1)
            ve.v += 1
    blk.vector(p8_vector)
    YLN_DONE = ve.v

    barrier()
    cut(6)
    if dbg and "d8" in dbg:
        def d8(e):
            for j in range(FC):
                e.dma_start(out=dbg_t["dbg_yl"][j * 128:(j + 1) * 128, :],
                            in_=yf[:, j, :]).then_inc(qs.h, 16)
                qs.v += 16
            e.wait_ge(qs.h, qs.v)
        blk.sync(d8)
        barrier()

    te9 = te.v
    ve9 = ve.v
    se9 = se.v

    def p9_v0(e):
        e.wait_ge(qg.h, WO_LD)
        e.tensor_copy(beta_bf[:], beta_sb[:]).then_inc(ve.h, 1)
        ve.v += 1
    blk.vector(p9_v0)
    BB_DONE = ve.v

    def p9_bo_t(e):
        e.wait_ge(ve.h, BB_DONE)
        for eg in range(2):
            for j in range(FC):
                ins = e.matmul(ps[0:1, 6 + eg, 0:512], beta_bf[:, j:j + 1],
                               wo_bf[:, j, eg * 512:(eg + 1) * 512],
                               start=(j == 0), stop=(j == FC - 1),
                               skip_group_check=True)
            ins.then_inc(te.h, 1)
            te.v += 1
    blk.tensor(p9_bo_t)
    BO_T = te.v

    def p9_v1(e):
        for eg in range(2):
            e.wait_ge(te.h, te9 + 1 + eg)
            e.tensor_copy(bo_sb[0:1, eg * 512:(eg + 1) * 512],
                          ps[0:1, 6 + eg, 0:512]).then_inc(ve.h, 1)
            ve.v += 1
        for j in range(FC):
            e.tensor_scalar(wo_bf[:, j, :], wo_bf[:, j, :],
                            gamma_sb[:, j:j + 1], None,
                            Alu.mult).then_inc(ve.h, 1)
            ve.v += 1
    blk.vector(p9_v1)
    WO_SCALED = ve.v

    def p9_mm(e):
        e.wait_ge(ve.h, WO_SCALED)
        k = 0
        for t in range(TC):
            for eg in range(2):
                k += 1
                b = (k - 1) % 4
                if k > 4:
                    e.wait_ge(se.h, se9 + (k - 4))
                for j in range(FC):
                    e.matmul(bank(b)[:],
                             yf[:, j, t * 128:(t + 1) * 128],
                             wo_bf[:, j, eg * 512:(eg + 1) * 512],
                             start=(j == 0), stop=False,
                             skip_group_check=True)
                e.matmul(bank(b)[:], obf1x_sb[:],
                         bo_sb[0:1, eg * 512:(eg + 1) * 512],
                         start=False, stop=True,
                         skip_group_check=True).then_inc(te.h, 1)
                te.v += 1
    blk.tensor(p9_mm)
    WO_MM_END = te.v

    def p9_ev(e):
        k = 0
        for t in range(TC):
            for eg in range(2):
                k += 1
                e.wait_ge(te.h, BO_T + k)
                e.activation(stok[:, t, eg * 512:(eg + 1) * 512],
                             bank((k - 1) % 4)[:], Act.Copy).then_inc(se.h, 1)
                se.v += 1
    blk.scalar(p9_ev)
    STOK_DONE = se.v

    barrier()
    cut(7)
    if dbg and "mid10" in dbg:
        def dm10(e):
            for j in range(FC):
                e.dma_start(out=dbg_t["dbg_y7"][j * 128:(j + 1) * 128, :],
                            in_=yf[:, j, :]).then_inc(qs.h, 16)
                qs.v += 16
            e.wait_ge(qs.h, qs.v)
        blk.sync(dm10)
        barrier()

    qs10 = qs.v
    ve10 = ve.v

    def p10_sync(e):
        e.wait_ge(qs.h, G2SP_END)
        for icx in range(IC):
            if icx >= 2:
                e.wait_ge(ve.h, ve10 + (icx - 2) * 2 + 2)
            e.dma_start(out=url[:, icx % 2, :],
                        in_=spill_up[icx * 128:(icx + 1) * 128, :]).then_inc(qs.h, 16)
            e.dma_start(out=g2rl[:, icx % 2, :],
                        in_=spill_g2[icx * 128:(icx + 1) * 128, :]).then_inc(qs.h, 16)
            qs.v += 32
    blk.sync(p10_sync)

    def p10_v(e):
        for icx in range(IC):
            e.wait_ge(qs.h, qs10 + (icx + 1) * 32)
            e.tensor_scalar(foq[:], url[:, icx % 2, :], srB_sb[:, 1:2], MAGIC,
                            Alu.mult, Alu.add)
            e.tensor_scalar(uq_bf[:], foq[:], -MAGIC, srB_sb[:, 0:1],
                            Alu.add, Alu.mult)
            e.tensor_scalar(foq[:], g2rl[:, icx % 2, :], srC_sb[:, 1:2],
                            MAGIC, Alu.mult, Alu.add)
            e.tensor_scalar(g2q_bf[:], foq[:], -MAGIC, srC_sb[:, 0:1],
                            Alu.add, Alu.mult)
            e.tensor_tensor(o_all[:, icx, :], uq_bf[:], g2q_bf[:],
                            Alu.mult).then_inc(ve.h, 1)
            ve.v += 1
            e.reduce_max(stat_sb[:, 10:11], o_all[:, icx, :], axis=AX,
                         apply_absolute_value=True)
            e.drain()
            if icx == 0:
                e.tensor_copy(macc_sb[:, 4:5], stat_sb[:, 10:11]).then_inc(ve.h, 1)
            else:
                e.tensor_tensor(macc_sb[:, 4:5], macc_sb[:, 4:5],
                                stat_sb[:, 10:11], Alu.max).then_inc(ve.h, 1)
            ve.v += 1
    blk.vector(p10_v)
    OPROC_V = ve.v

    def p10_dt(e):
        e.wait_ge(ve.h, OPROC_V)
        e.transpose(ps[0:1, 7, 0:128].bitcast(F32), macc_sb[:, 4:5],
                    eye_f_sb[:]).then_inc(te.h, 1)
        te.v += 1
    blk.tensor(p10_dt)
    D_TR = te.v

    def p10_dv(e):
        e.wait_ge(te.h, D_TR)
        e.reduce_max(mfin_sb[0:1, 32:33], ps[0:1, 7, 0:128],
                     axis=AX).then_inc(ve.h, 1)
        ve.v += 1
    blk.vector(p10_dv)
    D_SC = ve.v

    def p10_dag(e):
        e.wait_ge(ve.h, D_SC)
        e.dma_start(out=agd_i, in_=mfin_sb[0:1, 32:33]).then_inc(qg.h, 16)
        qg.v += 16
        e.wait_ge(qg.h, qg.v)
        e.collective_compute("AllGather", Alu.bypass, replica_groups=RG,
                             ins=[agd_i.opt()], outs=[agd_o.opt()]).then_inc(cc.h, 1)
        cc.v += 1
        e.wait_ge(cc.h, cc.v)
        e.dma_start(out=mfin_sb[0:1, 36:36 + NC], in_=agd_o.opt()).then_inc(qg.h, 16)
        qg.v += 16
    blk.gpsimd(p10_dag)
    CC_D = cc.v
    AGD_LD = qg.v

    def p10_dfin(e):
        e.wait_ge(qg.h, AGD_LD)
        e.reduce_max(mfin_sb[0:1, 32:33], mfin_sb[0:1, 36:36 + NC], axis=AX)
        e.drain()
        e.tensor_scalar(mfin_sb[0:1, 33:34], mfin_sb[0:1, 32:33], 1.0 / QMAX,
                        1e-12, Alu.mult, Alu.max)
        e.drain()
        e.reciprocal(mfin_sb[0:1, 34:35], mfin_sb[0:1, 33:34]).then_inc(ve.h, 1)
        ve.v += 1
    blk.vector(p10_dfin)
    D_FIN = ve.v

    def p10_dbt(e):
        e.wait_ge(ve.h, D_FIN)
        e.matmul(ps[:, 7, 0:2].bitcast(F32), of1x_sb[:], mfin_sb[0:1, 33:35],
                 start=True, stop=True).then_inc(te.h, 1)
        te.v += 1
    blk.tensor(p10_dbt)
    D_BC = te.v

    def p10_oq(e):
        e.wait_ge(te.h, D_BC)
        e.tensor_copy(srD_sb[:], ps[:, 7, 0:2])
        e.drain()
        e.tensor_copy(stat_sb[:, 11:12], stat_sb[:, 8:9]).then_inc(ve.h, 1)
        ve.v += 1
        for icx in range(IC):
            e.tensor_scalar(foq[:], o_all[:, icx, :], srD_sb[:, 1:2], MAGIC,
                            Alu.mult, Alu.add)
            e.tensor_scalar(o_all[:, icx, :], foq[:], -MAGIC, srD_sb[:, 0:1],
                            Alu.add, Alu.mult).then_inc(ve.h, 1)
            ve.v += 1
    blk.vector(p10_oq)
    OQ_DONE = ve.v

    barrier()
    cut(8)

    qs11 = qs.v
    te11 = te.v
    se11 = se.v
    NTB = max(TC // 4, 1)
    TBW = TC // NTB

    def p11_wload(e):
        for tb in range(NTB):
            for icx in range(IC):
                kk = tb * IC + icx
                if kk >= 2:
                    e.wait_ge(te.h, te11 + kk)
                e.dma_start(out=dnw[:, kk % 2, :],
                            in_=agdn_o[icx * 128:(icx + 1) * 128, :]).then_inc(qs.h, 16)
                qs.v += 16
    blk.sync(p11_wload)

    def p11_mm(e):
        if DLY_GTOK:
            e.nop(cycle_cnt=DLY_GTOK)
        for tb in range(NTB):
            if tb >= 1:
                e.wait_ge(se.h, se11 + tb * TBW * 2)
            for icx in range(IC):
                kk = tb * IC + icx
                e.wait_ge(qs.h, qs11 + (kk + 1) * 16)
                for tt in range(TBW):
                    t = tb * TBW + tt
                    for eg in range(2):
                        ins = e.matmul(ps[:, (tt * 2 + eg) % 8, :],
                                       o_all[:, icx, t * 128:(t + 1) * 128],
                                       dnw[:, kk % 2, eg * 512:(eg + 1) * 512],
                                       start=(icx == 0), stop=(icx == IC - 1),
                                       skip_group_check=True)
                ins.then_inc(te.h, 1)
                te.v += 1
    blk.tensor(p11_mm)

    def p11_ev(e):
        for tb in range(NTB):
            e.wait_ge(te.h, te11 + (tb + 1) * IC)
            if DLY_GTOK:
                e.nop(cycle_cnt=DLY_GTOK)
            for tt in range(TBW):
                t = tb * TBW + tt
                for eg in range(2):
                    e.activation(gtok[:, t, eg * 512:(eg + 1) * 512],
                                 ps[:, (tt * 2 + eg) % 8, :],
                                 Act.Silu).then_inc(se.h, 1)
                    se.v += 1
    blk.scalar(p11_ev)
    GTOK_DONE = se.v

    barrier()
    cut(9)

    qs12 = qs.v
    ve12 = ve.v
    OUT0 = qc.v

    def p12_xload(e):
        for t in range(TC):
            if t >= 2:
                e.wait_ge(ve.h, ve12 + (t - 2) * 2 + 2)
            e.dma_start(out=xrl[:, t % 2, :],
                        in_=x_in[t * 128:(t + 1) * 128, :]).then_inc(qs.h, 16)
            qs.v += 16
    blk.sync(p12_xload)

    def p12_v(e):
        for t in range(TC):
            e.wait_ge(qs.h, qs12 + (t + 1) * 16)
            if t >= 2:
                e.wait_ge(qc.h, OUT0 + (t - 1) * 16)
            e.tensor_tensor(sgt[:], stok[:, t, :], gtok[:, t, :],
                            Alu.mult).then_inc(ve.h, 1)
            ve.v += 1
            e.tensor_tensor(ost[:, t % 2, :], xrl[:, t % 2, :], sgt[:],
                            Alu.add).then_inc(ve.h, 1)
            ve.v += 1
    blk.vector(p12_v)

    def p12_st(e):
        for t in range(TC):
            e.wait_ge(ve.h, ve12 + t * 2 + 2)
            e.dma_start(out=out_d[t * 128:(t + 1) * 128, :],
                        in_=ost[:, t % 2, :]).then_inc(qc.h, 16)
            qc.v += 16
    blk.scalar(p12_st)
    OUT_DONE = qc.v

    def p12_fin(e):
        e.wait_ge(qc.h, OUT_DONE)
        if dbg and "late" in dbg:
            for j in range(FC):
                e.dma_start(out=dbg_t["dbg_y7"][j * 128:(j + 1) * 128, :],
                            in_=yf[:, j, :]).then_inc(qs.h, 16)
                qs.v += 16
            for t in range(TC):
                e.dma_start(out=dbg_t["dbg_s"][:, t, :],
                            in_=stok[:, t, :]).then_inc(qs.h, 16)
                e.dma_start(out=dbg_t["dbg_g"][:, t, :],
                            in_=gtok[:, t, :]).then_inc(qs.h, 16)
                qs.v += 32
            e.wait_ge(qs.h, qs.v)
        if dbg and "fin" in dbg:
            for t in range(TC):
                e.dma_start(out=dbg_t["dbg_s"][:, t, :],
                            in_=stok[:, t, :]).then_inc(qs.h, 16)
                e.dma_start(out=dbg_t["dbg_g"][:, t, :],
                            in_=gtok[:, t, :]).then_inc(qs.h, 16)
                qs.v += 32
            for i2 in range(IC):
                e.dma_start(out=dbg_t["dbg_o"][i2 * 128:(i2 + 1) * 128, :],
                            in_=o_all[:, i2, :]).then_inc(qs.h, 16)
                qs.v += 16
            e.dma_start(out=dbg_t["dbg_sr"][:, 8:10], in_=srD_sb[:]).then_inc(qs.h, 16)
            qs.v += 16
            e.wait_ge(qs.h, qs.v)
    blk.sync(p12_fin)

    blk_ctx.__exit__(None, None, None)
    nc._ledger = {"qs": qs.v, "qc": qc.v, "qv": qv.v, "qg": qg.v, "cc": cc.v,
                  "te": te.v, "ve": ve.v, "se": se.v, "barS": barS.v}
    return nc


_CACHE = {}


def _get_program(NC, T):
    key = (NC, T)
    if key not in _CACHE:
        _CACHE[key] = build_program(NC, T)
    return _CACHE[key]


def make_inputs(core, NC, T, x, Wq, Wk, Wv, Wg, Wo, gamma, beta, w_gate,
                w_up, w_down):
    b, h = core // 2, core % 2
    xs = np.ascontiguousarray(x[b, h * T:(h + 1) * T, :], dtype=np.float32)
    USH = DM // NC
    DSH = IM // NC
    NCH = T // CH
    sel4 = np.zeros((4, 128), np.float32)
    for j in range(4):
        sel4[j, j * 32:(j + 1) * 32] = 1.0
    trl = np.zeros((128, CH), np.float32)
    for p in range(128):
        trl[p, p % 64:] = 1.0
    trl = np.tile(trl, (1, FC))
    scanm = np.ones((128, T), np.float32)
    scanm[:, ::CH] = 0.0
    selh = np.zeros((128, NC), np.float32)
    if h == 1:
        selh[:, core - 1] = 1.0
    return {
        "x_in": xs,
        "wq_t": np.ascontiguousarray(Wq.T, np.float32),
        "wk_t": np.ascontiguousarray(Wk.T, np.float32),
        "wv_t": np.ascontiguousarray(Wv.T, np.float32),
        "wg_t": np.ascontiguousarray(Wg.T, np.float32),
        "wo_t": np.ascontiguousarray(Wo.T, np.float32),
        "gamma_t": np.ascontiguousarray(gamma.reshape(FC, 128).T, np.float32),
        "beta_t": np.ascontiguousarray(beta.reshape(FC, 128).T, np.float32),
        "wup_s": np.ascontiguousarray(w_up[:, core * USH:(core + 1) * USH],
                                      np.float32),
        "wgt_s": np.ascontiguousarray(w_gate[:, core * USH:(core + 1) * USH],
                                      np.float32),
        "wdn_s": np.ascontiguousarray(w_down[:, core * DSH:(core + 1) * DSH],
                                      np.float32),
        "wup_ts": np.ascontiguousarray(w_up.T[core * USH:(core + 1) * USH, :],
                                       np.float32),
        "wgt_ts": np.ascontiguousarray(
            w_gate.T[core * USH:(core + 1) * USH, :], np.float32),
        "wdn_ts": np.ascontiguousarray(
            w_down.T[core * DSH:(core + 1) * DSH, :], np.float32),
        "eye_f": np.eye(128, dtype=np.float32),
        "sel4": sel4,
        "tril": trl.astype(bfloat16),
        "scanm": scanm,
        "onesch": np.ones((128, NCH), np.float32),
        "ones_f_1x": np.ones((1, 128), np.float32),
        "ones_bf_1x": np.ones((1, 128), bfloat16),
        "ones_bf_c1": np.ones((128, 1), bfloat16),
        "selH": selh,
    }


def kernel(L0_attn_x, L0_attn_Wq, L0_attn_Wk, L0_attn_Wv, L0_attn_Wg,
           L0_attn_Wo, L0_attn_gamma, L0_attn_beta, L0_ffn_w_gate,
           L0_ffn_w_up, L0_ffn_w_down):
    NC = 8
    x = np.asarray(L0_attn_x, dtype=np.float32)
    B, S, _ = x.shape
    T = (B * S) // NC
    nc = _get_program(NC, T)
    args = [np.asarray(a, dtype=np.float32) for a in
            (L0_attn_Wq, L0_attn_Wk, L0_attn_Wv, L0_attn_Wg, L0_attn_Wo,
             L0_attn_gamma, L0_attn_beta, L0_ffn_w_gate, L0_ffn_w_up,
             L0_ffn_w_down)]
    in_maps = [make_inputs(c, NC, T, x, *args) for c in range(NC)]
    res = bass_utils.run_bass_kernel_spmd(nc, in_maps,
                                          core_ids=list(range(NC)))
    out = np.empty((B, S, DM), np.float32)
    for c in range(NC):
        b, h = c // 2, c % 2
        out[b, h * T:(h + 1) * T, :] = res.results[c]["out"]
    return out


# version 08 (final)
# speedup vs baseline: 1.1077x; 1.1077x over previous
import numpy as np
import ml_dtypes
import concourse.bass as bass
import concourse.mybir as mybir
import concourse.bass_utils as bass_utils

F32 = mybir.dt.float32
BF16 = mybir.dt.bfloat16
Alu = mybir.AluOpType
Act = mybir.ActivationFunctionType
AX = mybir.AxisListType.X
bfloat16 = ml_dtypes.bfloat16

MAGIC = 1.5 * 2.0**23
DLY_STAT = 6000
DLY_RESC = 6000
DLY_GTOK = 6000
DLY_YADD = 0
NO_KTR = False
NO_SCAN = False
NO_INPLACE = False
P3_LVL = 2
QMAX = 32767.0
LN_EPS = 1e-5
DM = 1024
IM = 3072
HD = 64
CH = 64
FC = 8
IC = IM // 128


class SemC:
    def __init__(self, nc, name):
        self.h = nc.alloc_semaphore(name)
        self.v = 0


class _Cut(Exception):
    pass


def build_program(NC=8, T=2048, dbg=False, stop=None):
    try:
        return _build(NC, T, dbg, stop)
    except _Cut as c:
        return c.args[0]


def _build(NC=8, T=2048, dbg=False, stop=None):
    nc = bass.Bass("TRN2", target_bir_lowering=False, debug=False,
                   detect_race_conditions=False)
    TC = T // 128
    NTG = max(T // 512, 1)
    TGW = min(T, 512)
    NCH = T // CH
    USH = DM // NC
    UIC = USH // 128
    DSH = IM // NC
    DIC = DSH // 128

    def din(name, shape, dt=F32):
        return nc.dram_tensor(name, shape, dt, kind="ExternalInput").ap()

    x_in = din("x_in", [T, DM])
    wq_t = din("wq_t", [DM, DM])
    wk_t = din("wk_t", [DM, DM])
    wv_t = din("wv_t", [DM, DM])
    wg_t = din("wg_t", [DM, DM])
    wo_t = din("wo_t", [DM, DM])
    gamma_t = din("gamma_t", [128, FC])
    beta_t = din("beta_t", [128, FC])
    wup_s = din("wup_s", [IM, USH])
    wgt_s = din("wgt_s", [IM, USH])
    wdn_s = din("wdn_s", [DM, DSH])
    wup_ts = din("wup_ts", [USH, IM])
    wgt_ts = din("wgt_ts", [USH, IM])
    wdn_ts = din("wdn_ts", [DSH, DM])
    eye_f = din("eye_f", [128, 128])
    sel4 = din("sel4", [4, 128])
    tril = din("tril", [128, FC * CH], BF16)
    scanm = din("scanm", [128, T])
    onesch = din("onesch", [128, NCH])
    ones_f_1x = din("ones_f_1x", [1, 128])
    ones_bf_1x = din("ones_bf_1x", [1, 128], BF16)
    ones_bf_c1 = din("ones_bf_c1", [128, 1], BF16)
    selH = din("selH", [128, NC])

    out_d = nc.dram_tensor("out", [T, DM], F32, kind="ExternalOutput").ap()

    def dint(name, shape, dt):
        return nc.dram_tensor(name, shape, dt, kind="Internal").ap()

    spill_up = dint("spill_up", [IM, T], BF16)
    spill_gt = dint("spill_gt", [IM, T], BF16)
    spill_g2 = dint("spill_g2", [IM, T], BF16)
    spill_v = dint("spill_v", [T, DM], BF16)
    agup_i = dint("agup_i", [USH, IM], BF16)
    agup_o = dint("agup_o", [DM, IM], BF16)
    aggt_i = dint("aggt_i", [USH, IM], BF16)
    aggt_o = dint("aggt_o", [DM, IM], BF16)
    agdn_i = dint("agdn_i", [DSH, DM], BF16)
    agdn_o = dint("agdn_o", [IM, DM], BF16)
    aga_i = dint("aga_i", [1, 1], F32)
    aga_o = dint("aga_o", [NC, 1], F32)
    agh_i = dint("agh_i", [129, 512], F32)
    agh_o = dint("agh_o", [129 * NC, 512], F32)
    agc_i = dint("agc_i", [1, 1], F32)
    agc_o = dint("agc_o", [NC, 1], F32)
    agd_i = dint("agd_i", [1, 1], F32)
    agd_o = dint("agd_o", [NC, 1], F32)
    dbg_t = {}
    if dbg:
        for nm, shape, dt in [("dbg_xq", [FC * 128, T], BF16),
                              ("dbg_qf", [FC * 128, T], BF16),
                              ("dbg_kf", [FC * 128, T], BF16),
                              ("dbg_v", [T, DM], BF16),
                              ("dbg_y", [FC * 128, T], BF16),
                              ("dbg_yl", [FC * 128, T], BF16),
                              ("dbg_s", [128, TC, DM], BF16),
                              ("dbg_g", [128, TC, DM], BF16),
                              ("dbg_o", [IC * 128, T], BF16),
                              ("dbg_up", [IM, T], BF16),
                              ("dbg_gt", [IM, T], BF16),
                              ("dbg_g2", [IM, T], BF16),
                              ("dbg_sr", [128, 12], F32),
                              ("dbg_wup", [DM, IM], BF16),
                              ("dbg_wdn", [IM, DM], BF16),
                              ("dbg_h", [128, FC * HD], F32),
                              ("dbg_y7", [FC * 128, T], BF16),
                              ("dbg_hin", [128, FC * HD], F32),
                              ("dbg_ec", [128, FC * NCH], F32),
                              ("dbg_e2", [128, FC * NCH], F32)]:
            dbg_t[nm] = nc.dram_tensor(nm, shape, dt,
                                       kind="ExternalOutput").ap()

    AR = 212800
    arena = nc.alloc_sbuf_tensor("arena", [128, AR], mybir.dt.uint8)
    base = nc.lookup_mloc(arena).addr
    cur = [0]

    def sb(name, shape, dt, at=None):
        nbytes = int(np.prod(shape[1:])) * mybir.dt.size(dt)
        if at is None:
            at = cur[0]
            cur[0] = at + ((nbytes + 31) & ~31)
        assert at + nbytes <= AR, f"{name}: arena overflow {at + nbytes}"
        return nc.alloc_sbuf_tensor_at(name, shape, dt, offset=base + at)

    eye_f_sb = sb("eye_f_sb", [128, 128], F32)
    sel4_sb = sb("sel4_sb", [4, 128], F32)
    tril_sb = sb("tril_sb", [128, FC * CH], BF16)
    onesch_sb = sb("onesch_sb", [128, NCH], F32)
    of1x_sb = sb("of1x_sb", [1, 128], F32)
    obf1x_sb = sb("obf1x_sb", [1, 128], BF16)
    obfc1_sb = sb("obfc1_sb", [128, 1], BF16)
    gamma_sb = sb("gamma_sb", [128, FC], F32)
    beta_sb = sb("beta_sb", [128, FC], F32)
    beta_bf = sb("beta_bf", [128, FC], BF16)
    selH_sb = sb("selH_sb", [128, NC], F32)
    lcc_sb = sb("lcc_sb", [128, FC, NCH], F32)
    ec_sb = sb("ec_sb", [128, FC, NCH], F32)
    macc_sb = sb("macc_sb", [128, 8], F32)
    m2_sb = sb("m2_sb", [2, 1], F32)
    mfin_sb = sb("mfin_sb", [1, 64], F32)
    srA_sb = sb("srA_sb", [128, 2], F32)
    srB_sb = sb("srB_sb", [128, 4], F32)
    srC_sb = sb("srC_sb", [128, 2], F32)
    srD_sb = sb("srD_sb", [128, 2], F32)
    stat_sb = sb("stat_sb", [128, 16], F32)
    SALL_N = max((IM // 128) * (USH // 32), (DM // 128) * (DSH // 32))
    sall_sb = sb("sall_sb", [128, SALL_N], F32)
    rall_sb = sb("rall_sb", [128, SALL_N], F32)

    atA = cur[0]; xn = sb("xn", [128, FC, T], BF16)
    atB = cur[0]; qf = sb("qf", [128, FC, T], BF16)
    atC = cur[0]; kf = sb("kf", [128, FC, T], BF16)
    atE = cur[0]; ktok = sb("ktok", [128, TC, DM], BF16)
    stok = sb("stok", [128, TC, DM], BF16, at=atE)
    atF = cur[0]; scanm_sb = sb("scanm_sb", [128, T], F32)
    yf = sb("yf", [128, FC, T], BF16, at=atF)
    gtok = sb("gtok", [128, TC, DM], BF16, at=atF)
    cur[0] = atF + max(T * 4, FC * T * 2, TC * DM * 2)
    o_all = sb("o_all", [128, IC, T], BF16, at=atA)
    assert IC * T * 2 <= (atE - atA), "o_all overlay too big"

    atS = cur[0]
    cur[0] += 40 * 1024
    assert cur[0] <= AR, f"arena used {cur[0]}"

    def scr(name, shape, dt, off):
        nbytes = int(np.prod(shape[1:])) * mybir.dt.size(dt)
        assert off + nbytes <= 40 * 1024, f"{name} scratch overflow"
        return nc.alloc_sbuf_tensor_at(name, shape, dt, offset=base + atS + off)

    wdq = scr("wdq", [128, IM], F32, 0)
    sT_sc = scr("sT_sc", [4, IM], F32, 12 * 1024)
    rT_sc = scr("rT_sc", [4, IM], F32, 24 * 1024)
    wqb = scr("wqb", [128, 2, 512], BF16, 36 * 1024)
    fdq = scr("fdq", [128, 512], F32, 38 * 1024)
    xld = scr("xld", [128, 2, DM], F32, 0)
    xnt = scr("xnt", [128, 2, DM], BF16, 8 * 1024)
    sqs = scr("sqs", [128, DM], F32, 12 * 1024)
    wbuf = scr("wbuf", [128, FC, DM], BF16, 0)
    prep_a = scr("prep_a", [128, T], F32, 16 * 1024)
    prep_c = scr("prep_c", [128, T], F32, 24 * 1024)
    hld_sb = scr("hld_sb", [128, 2, 512], F32, 0)
    zrow_sb = scr("zrow_sb", [1, 512], F32, 0)
    vbuf_sb = scr("vbuf_sb", [128, 2, DM], BF16, 4 * 1024)
    h_sb = scr("h_sb", [128, FC, HD], F32, 8 * 1024)
    sm_sb = scr("sm_sb", [128, 2, 512], BF16, 10 * 1024)
    g0_sb = scr("g0_sb", [128, FC, HD], BF16, 12 * 1024)
    qhat_sb = scr("qhat_sb", [128, FC, HD], BF16, 13 * 1024)
    hin_bf = scr("hin_bf", [128, FC, HD], BF16, 14 * 1024)
    hin_sb = scr("hin_sb", [128, FC, HD], F32, 15 * 1024)
    e2_sb = scr("e2_sb", [128, FC, NCH], F32, 17 * 1024)
    bo_sb = scr("bo_sb", [1, DM], BF16, 18 * 1024)
    fxq = scr("fxq", [128, T], F32, 32 * 1024)
    vstg = scr("vstg", [128, 2, DM], BF16, 32 * 1024)
    ugw = scr("ugw", [128, 2, FC, 128], BF16, 0)
    ustg = scr("ustg", [128, 2, T], BF16, 4 * 1024)
    fscA = scr("fscA", [128, T], F32, 4 * 1024)
    fscB = scr("fscB", [128, T], F32, 12 * 1024)
    gq_bf = scr("gq_bf", [128, T], BF16, 20 * 1024)
    g2st = scr("g2st", [128, 2, T], BF16, 24 * 1024)
    grl = scr("grl", [128, 2, T], BF16, 32 * 1024)
    wo_bf = scr("wo_bf", [128, FC, DM], BF16, 0)
    y2s = scr("y2s", [128, FC, 512], BF16, 16 * 1024)
    mu_g = scr("mu_g", [1, 512], F32, 24 * 1024)
    rl_g = scr("rl_g", [1, 512], F32, 26 * 1024)
    vv_g = scr("vv_g", [1, 512], F32, 28 * 1024)
    nw_g = scr("nw_g", [1, 512], F32, 30 * 1024)
    url = scr("url", [128, 2, T], BF16, 0)
    g2rl = scr("g2rl", [128, 2, T], BF16, 8 * 1024)
    foq = scr("foq", [128, T], F32, 16 * 1024)
    uq_bf = scr("uq_bf", [128, T], BF16, 24 * 1024)
    g2q_bf = scr("g2q_bf", [128, T], BF16, 28 * 1024)
    dnw = scr("dnw", [128, 3, DM], BF16, 32 * 1024)
    xrl = scr("xrl", [128, 2, DM], F32, 0)
    sgt = scr("sgt", [128, DM], BF16, 8 * 1024)
    ost = scr("ost", [128, 2, DM], F32, 12 * 1024)

    ps = nc.alloc_psum_tensor("ps", [128, 8, 512], F32)

    def bank(i):
        return ps[:, i, :]

    qs = SemC(nc, "qs")
    qc = SemC(nc, "qc")
    qv = SemC(nc, "qv")
    qg = SemC(nc, "qg")
    cc = SemC(nc, "cc")
    te = SemC(nc, "te")
    ve = SemC(nc, "ve")
    se = SemC(nc, "se")
    barS = SemC(nc, "barS")

    blk_ctx = nc.Block()
    blk = blk_ctx.__enter__()

    def barrier():
        barS.v += 5
        tgt = barS.v

        def mk(e):
            e.drain()
            e.nop().then_inc(barS.h, 1)
            e.wait_ge(barS.h, tgt)
        blk.tensor(mk)
        blk.vector(mk)
        blk.scalar(mk)
        blk.gpsimd(mk)
        blk.sync(mk)

    RG = [list(range(NC))]

    def cut(n):
        if stop is not None and stop == n:
            def drain_s(e):
                e.wait_ge(qs.h, qs.v)
            def drain_c(e):
                e.wait_ge(qc.h, qc.v)
            def drain_g(e):
                e.wait_ge(qg.h, qg.v)
                e.wait_ge(cc.h, cc.v)
            blk.sync(drain_s)
            blk.scalar(drain_c)
            blk.gpsimd(drain_g)
            barrier()
            blk_ctx.__exit__(None, None, None)
            raise _Cut(nc)

    def p0_loads(e):
        for dst, src in [(eye_f_sb, eye_f), (sel4_sb, sel4), (tril_sb, tril),
                         (onesch_sb, onesch), (of1x_sb, ones_f_1x),
                         (obf1x_sb, ones_bf_1x), (obfc1_sb, ones_bf_c1),
                         (gamma_sb, gamma_t), (beta_sb, beta_t),
                         (selH_sb, selH), (scanm_sb, scanm)]:
            e.dma_start(out=dst[:], in_=src).then_inc(qs.h, 16)
            qs.v += 16
    blk.sync(p0_loads)
    CONST_DONE = qs.v

    prev_dq_done = [0]

    def dequant2(st_d, tr_d, ag_in, O, SHW, nic):
        noc = O // 128
        nblk = SHW // 32
        sW = sall_sb[:, 0:noc * nblk].rearrange("p (a b) -> p a b", b=nblk)
        rW = rall_sb[:, 0:noc * nblk].rearrange("p (a b) -> p a b", b=nblk)
        ld0 = qs.v
        vb0 = ve.v
        PREV = prev_dq_done[0]

        def ld(e):
            e.wait_ge(qs.h, CONST_DONE)
            e.wait_ge(ve.h, PREV)
            for oc in range(noc):
                if oc >= 1:
                    e.wait_ge(ve.h, vb0 + oc)
                e.dma_start(out=wdq[:, 0:SHW],
                            in_=st_d[oc * 128:(oc + 1) * 128, :]).then_inc(qs.h, 16)
                qs.v += 16
        blk.sync(ld)

        def vmax(e):
            for oc in range(noc):
                e.wait_ge(qs.h, ld0 + (oc + 1) * 16)
                src3 = wdq[:, 0:SHW].rearrange("p (b w) -> p b w", w=32)
                e.reduce_max(sW[:, oc, :], src3, axis=AX,
                             apply_absolute_value=True)
                e.drain()
                e.tensor_scalar(sW[:, oc, :], sW[:, oc, :], 1.0 / 7.0, 1e-12,
                                Alu.mult, Alu.max)
                e.drain()
                e.reciprocal(rW[:, oc, :], sW[:, oc, :]).then_inc(ve.h, 1)
                ve.v += 1
        blk.vector(vmax)
        SC_DONE = ve.v

        for icx in range(nic):
            tb0 = te.v
            TCP_PREV = ve.v

            def ttr(e, icx=icx, tb0=tb0, TCP_PREV=TCP_PREV):
                e.wait_ge(ve.h, SC_DONE)
                if icx >= 1:
                    e.wait_ge(ve.h, TCP_PREV)
                for oc in range(noc):
                    if oc >= 1:
                        e.wait_ge(ve.h, TCP_PREV + oc)
                    e.transpose(ps[0:4, 4, 0:128].bitcast(F32),
                                sW[:, oc, 4 * icx:4 * icx + 4], eye_f_sb[:])
                    e.transpose(ps[0:4, 5, 0:128].bitcast(F32),
                                rW[:, oc, 4 * icx:4 * icx + 4],
                                eye_f_sb[:]).then_inc(te.h, 1)
                    te.v += 1
            blk.tensor(ttr)

            def tcp(e, icx=icx, tb0=tb0):
                for oc in range(noc):
                    e.wait_ge(te.h, tb0 + oc + 1)
                    e.tensor_copy(sT_sc[0:4, oc * 128:(oc + 1) * 128],
                                  ps[0:4, 4, 0:128])
                    e.tensor_copy(rT_sc[0:4, oc * 128:(oc + 1) * 128],
                                  ps[0:4, 5, 0:128]).then_inc(ve.h, 1)
                    ve.v += 1
            blk.vector(tcp)
            TCP_DONE = ve.v

            ld2b = qs.v

            def ld2(e, icx=icx, TCP_DONE=TCP_DONE):
                e.wait_ge(ve.h, TCP_DONE)
                e.dma_start(out=wdq[:, 0:O],
                            in_=tr_d[icx * 128:(icx + 1) * 128, 0:O]).then_inc(qs.h, 16)
                qs.v += 16
            blk.sync(ld2)
            LD2_DONE = qs.v

            nog = O // 512
            mm0 = te.v
            DQ0 = ve.v
            DQG0 = qg.v

            def selmm(e, mm0=mm0, DQ0=DQ0, nog=nog, TCP_DONE=TCP_DONE):
                e.wait_ge(ve.h, TCP_DONE)
                for og in range(nog):
                    if og >= 1:
                        e.wait_ge(ve.h, DQ0 + og)
                    e.matmul(bank(6), sel4_sb[:],
                             sT_sc[0:4, og * 512:(og + 1) * 512],
                             start=True, stop=True)
                    e.matmul(bank(7), sel4_sb[:],
                             rT_sc[0:4, og * 512:(og + 1) * 512],
                             start=True, stop=True).then_inc(te.h, 1)
                    te.v += 1
            blk.tensor(selmm)

            def dq(e, icx=icx, mm0=mm0, LD2_DONE=LD2_DONE, DQG0=DQG0, nog=nog):
                e.wait_ge(qs.h, LD2_DONE)
                for og in range(nog):
                    e.wait_ge(te.h, mm0 + og + 1)
                    if og >= 2:
                        e.wait_ge(qg.h, DQG0 + (og - 1) * 16)
                    w = wdq[:, og * 512:(og + 1) * 512]
                    o = wqb[:, og % 2, :]
                    e.tensor_tensor(fdq[:], w, bank(7), Alu.mult)
                    e.tensor_scalar(fdq[:], fdq[:], MAGIC, None, Alu.add)
                    e.scalar_tensor_tensor(o, fdq[:], -MAGIC, bank(6),
                                           Alu.add, Alu.mult).then_inc(ve.h, 1)
                    ve.v += 1
            blk.vector(dq)
            DQ_DONE = ve.v

            def st(e, icx=icx, DQ0=DQ0, nog=nog):
                for og in range(nog):
                    e.wait_ge(ve.h, DQ0 + og + 1)
                    e.dma_start(out=ag_in[icx * 128:(icx + 1) * 128,
                                          og * 512:(og + 1) * 512],
                                in_=wqb[:, og % 2, :]).then_inc(qg.h, 16)
                    qg.v += 16
            blk.gpsimd(st)
            prev_dq_done[0] = DQ_DONE
        return qg.v

    dequant2(wup_s, wup_ts, agup_i, IM, USH, UIC)
    dequant2(wgt_s, wgt_ts, aggt_i, IM, USH, UIC)
    DQ_ALL = dequant2(wdn_s, wdn_ts, agdn_i, DM, DSH, DIC)

    def p0_ag(e):
        e.wait_ge(qg.h, DQ_ALL)
        e.collective_compute("AllGather", Alu.bypass, replica_groups=RG,
                             ins=[agup_i.opt()], outs=[agup_o.opt()]).then_inc(cc.h, 1)
        e.collective_compute("AllGather", Alu.bypass, replica_groups=RG,
                             ins=[aggt_i.opt()], outs=[aggt_o.opt()]).then_inc(cc.h, 1)
        e.collective_compute("AllGather", Alu.bypass, replica_groups=RG,
                             ins=[agdn_i.opt()], outs=[agdn_o.opt()]).then_inc(cc.h, 1)
        cc.v += 3
    blk.gpsimd(p0_ag)
    CC_W = cc.v

    barrier()
    cut(1)

    P1_LD0 = qs.v
    se1 = se.v
    ve1 = ve.v
    qc1 = qc.v

    def p1_load(e):
        e.wait_ge(qs.h, CONST_DONE)
        for t in range(TC):
            if t >= 2:
                e.wait_ge(ve.h, ve1 + (t - 2) * 3 + 3)
            e.dma_start(out=xld[:, t % 2, :],
                        in_=x_in[t * 128:(t + 1) * 128, :]).then_inc(qs.h, 16)
            qs.v += 16
    blk.sync(p1_load)

    def p1_scalar(e):
        for t in range(TC):
            e.wait_ge(qs.h, P1_LD0 + (t + 1) * 16)
            e.square(sqs[:], xld[:, t % 2, :]).then_inc(se.h, 1)
            se.v += 1
            e.wait_ge(ve.h, ve1 + t * 3 + 2)
            e.sqrt(stat_sb[:, 3:4], stat_sb[:, 1:2]).then_inc(se.h, 1)
            se.v += 1
            e.wait_ge(ve.h, ve1 + t * 3 + 3)
            for j in range(FC):
                e.dma_start_transpose(
                    xn[:, j, t * 128:(t + 1) * 128],
                    xnt[:, t % 2, j * 128:(j + 1) * 128]).then_inc(qc.h, 16)
            qc.v += FC * 16
    blk.scalar(p1_scalar)
    XN_T_DONE = qc.v

    def p1_vector(e):
        for t in range(TC):
            e.wait_ge(qs.h, P1_LD0 + (t + 1) * 16)
            x_c = xld[:, t % 2, :]
            e.reduce_sum(stat_sb[:, 0:1], x_c, axis=AX)
            e.drain()
            e.tensor_scalar(stat_sb[:, 0:1], stat_sb[:, 0:1], 1.0 / DM, None,
                            Alu.mult).then_inc(ve.h, 1)
            ve.v += 1
            e.wait_ge(se.h, se1 + t * 2 + 1)
            e.reduce_sum(stat_sb[:, 1:2], sqs[:], axis=AX)
            e.drain()
            e.tensor_scalar(stat_sb[:, 1:2], stat_sb[:, 1:2], 1.0 / DM, None,
                            Alu.mult)
            e.tensor_tensor(stat_sb[:, 2:3], stat_sb[:, 0:1], stat_sb[:, 0:1],
                            Alu.mult)
            e.drain()
            e.tensor_tensor(stat_sb[:, 1:2], stat_sb[:, 1:2], stat_sb[:, 2:3],
                            Alu.subtract)
            e.drain()
            e.tensor_scalar(stat_sb[:, 1:2], stat_sb[:, 1:2], LN_EPS, None,
                            Alu.add).then_inc(ve.h, 1)
            ve.v += 1
            e.wait_ge(se.h, se1 + t * 2 + 2)
            e.reciprocal(stat_sb[:, 4:5], stat_sb[:, 3:4])
            e.drain()
            e.tensor_tensor(stat_sb[:, 5:6], stat_sb[:, 4:5], stat_sb[:, 4:5],
                            Alu.mult)
            e.drain()
            e.tensor_tensor(stat_sb[:, 5:6], stat_sb[:, 5:6], stat_sb[:, 1:2],
                            Alu.mult)
            e.drain()
            e.tensor_scalar(stat_sb[:, 5:6], stat_sb[:, 5:6], -0.5, 1.5,
                            Alu.mult, Alu.add)
            e.drain()
            e.tensor_tensor(stat_sb[:, 4:5], stat_sb[:, 4:5], stat_sb[:, 5:6],
                            Alu.mult)
            e.drain()
            if t >= 2:
                e.wait_ge(qc.h, qc1 + (t - 1) * FC * 16)
            e.tensor_scalar(xnt[:, t % 2, :], x_c, stat_sb[:, 0:1],
                            stat_sb[:, 4:5], Alu.subtract, Alu.mult)
            e.reduce_max(stat_sb[:, 6:7], xnt[:, t % 2, :], axis=AX,
                         apply_absolute_value=True)
            e.drain()
            if t == 0:
                e.tensor_copy(macc_sb[:, 0:1], stat_sb[:, 6:7]).then_inc(ve.h, 1)
            else:
                e.tensor_tensor(macc_sb[:, 0:1], macc_sb[:, 0:1],
                                stat_sb[:, 6:7], Alu.max).then_inc(ve.h, 1)
            ve.v += 1
    blk.vector(p1_vector)
    VE_P1 = ve.v

    def p1_af_t(e):
        e.wait_ge(ve.h, VE_P1)
        e.transpose(ps[0:1, 7, 0:128].bitcast(F32), macc_sb[:, 0:1],
                    eye_f_sb[:]).then_inc(te.h, 1)
        te.v += 1
    blk.tensor(p1_af_t)
    A_TR = te.v

    def p1_af_v(e):
        e.wait_ge(te.h, A_TR)
        e.reduce_max(mfin_sb[0:1, 0:1], ps[0:1, 7, 0:128], axis=AX).then_inc(ve.h, 1)
        ve.v += 1
    blk.vector(p1_af_v)
    A_SC = ve.v

    def p1_ag(e):
        e.wait_ge(ve.h, A_SC)
        e.dma_start(out=aga_i, in_=mfin_sb[0:1, 0:1]).then_inc(qg.h, 16)
        qg.v += 16
        e.wait_ge(qg.h, qg.v)
        e.collective_compute("AllGather", Alu.bypass, replica_groups=RG,
                             ins=[aga_i.opt()], outs=[aga_o.opt()]).then_inc(cc.h, 1)
        cc.v += 1
        e.wait_ge(cc.h, cc.v)
        e.dma_start(out=mfin_sb[0:1, 8:8 + NC], in_=aga_o.opt()).then_inc(qg.h, 16)
        qg.v += 16
    blk.gpsimd(p1_ag)
    CC_A = cc.v
    AGA_LD = qg.v

    def p2_v(e):
        e.wait_ge(qg.h, AGA_LD)
        e.reduce_max(mfin_sb[0:1, 0:1], mfin_sb[0:1, 8:8 + NC], axis=AX)
        e.drain()
        e.tensor_scalar(mfin_sb[0:1, 1:2], mfin_sb[0:1, 0:1], 1.0 / QMAX,
                        1e-12, Alu.mult, Alu.max)
        e.drain()
        e.reciprocal(mfin_sb[0:1, 2:3], mfin_sb[0:1, 1:2]).then_inc(ve.h, 1)
        ve.v += 1
    blk.vector(p2_v)
    P2_V = ve.v

    def p2_t(e):
        e.wait_ge(ve.h, P2_V)
        e.matmul(ps[:, 7, 0:2].bitcast(F32), of1x_sb[:], mfin_sb[0:1, 1:3],
                 start=True, stop=True).then_inc(te.h, 1)
        te.v += 1
    blk.tensor(p2_t)
    P2_T = te.v

    def p2_v2(e):
        e.wait_ge(te.h, P2_T)
        e.tensor_copy(srA_sb[:], ps[:, 7, 0:2]).then_inc(ve.h, 1)
        ve.v += 1
    blk.vector(p2_v2)

    barrier()
    cut(2)

    projw = [wq_t, wk_t, wv_t, wg_t]
    qg3 = qg.v
    te3 = te.v
    NQK = 2 * FC * NTG
    NV = TC * 2
    te_q_done = te3 + FC * NTG
    te_k_done = te3 + NQK
    te_v_done = te3 + NQK + NV
    te_g_done = te3 + NQK + NV + FC * NTG
    SPJ = NTG + 3

    def p3_wload(e):
        e.wait_ge(qc.h, XN_T_DONE)
        e.wait_ge(ve.h, VE_P1)
        for i in range(4):
            if i == 1:
                e.wait_ge(te.h, te_q_done)
            if i == 2:
                e.wait_ge(te.h, te_k_done)
            if i == 3:
                e.wait_ge(te.h, te_v_done)
            w = projw[i]
            for j in range(FC):
                e.dma_start(out=wbuf[:, j, :],
                            in_=w[j * 128:(j + 1) * 128, :]).then_inc(qg.h, 16)
            qg.v += FC * 16
    blk.gpsimd(p3_wload)

    ve3 = ve.v
    se3 = se.v

    def p3_mm(e):
        e.wait_ge(qc.h, XN_T_DONE)
        k = 0
        for i in range(2):
            e.wait_ge(qg.h, qg3 + (i + 1) * FC * 16)
            for j in range(FC):
                for g in range(NTG):
                    k += 1
                    b = (k - 1) % 4
                    if k > 4:
                        e.wait_ge(ve.h, ve3 + (k - 4))
                    for jin in range(FC):
                        ins = e.matmul(bank(b)[:, 0:TGW],
                                       wbuf[:, jin, j * 128:(j + 1) * 128],
                                       xn[:, jin, g * TGW:(g + 1) * TGW],
                                       start=(jin == 0), stop=(jin == FC - 1))
                    ins.then_inc(te.h, 1)
                    te.v += 1
        e.wait_ge(qg.h, qg3 + 3 * FC * 16)
        for t in range(TC):
            for g in range(2):
                k += 1
                b = (k - 1) % 4
                e.wait_ge(ve.h, ve3 + (k - 4))
                for jin in range(FC):
                    ins = e.matmul(bank(b)[:],
                                   xn[:, jin, t * 128:(t + 1) * 128],
                                   wbuf[:, jin, g * 512:(g + 1) * 512],
                                   start=(jin == 0), stop=(jin == FC - 1))
                ins.then_inc(te.h, 1)
                te.v += 1
        if P3_LVL < 1:
            return
        e.wait_ge(qg.h, qg3 + 4 * FC * 16)
        for j in range(FC):
            for g in range(NTG):
                k += 1
                b = (k - 1) % 4
                kg = j * NTG + g
                if kg >= 4:
                    jj, gg = divmod(kg - 4, NTG)
                    e.wait_ge(se.h, se3 + jj * SPJ + gg + 1)
                else:
                    e.wait_ge(ve.h, ve3 + NQK + NV)
                for jin in range(FC):
                    ins = e.matmul(bank(b)[:, 0:TGW],
                                   wbuf[:, jin, j * 128:(j + 1) * 128],
                                   xn[:, jin, g * TGW:(g + 1) * TGW],
                                   start=(jin == 0), stop=(jin == FC - 1))
                ins.then_inc(te.h, 1)
                te.v += 1
    blk.tensor(p3_mm)

    VSP0 = qs.v

    def p3_ev(e):
        k = 0
        for i in range(2):
            dst = qf if i == 0 else kf
            for j in range(FC):
                for g in range(NTG):
                    k += 1
                    e.wait_ge(te.h, te3 + k)
                    e.tensor_copy(dst[:, j, g * TGW:(g + 1) * TGW],
                                  bank((k - 1) % 4)[:, 0:TGW]).then_inc(ve.h, 1)
                    ve.v += 1
        for t in range(TC):
            for g in range(2):
                k += 1
                e.wait_ge(te.h, te3 + k)
                if t >= 2 and g == 0:
                    e.wait_ge(qs.h, VSP0 + (t - 1) * 16)
                e.tensor_copy(vstg[:, t % 2, g * 512:(g + 1) * 512],
                              bank((k - 1) % 4)[:]).then_inc(ve.h, 1)
                ve.v += 1
    blk.vector(p3_ev)
    VE_EV = ve.v

    VSP0 = qs.v

    def p3_vspill(e):
        for t in range(TC):
            e.wait_ge(ve.h, ve3 + NQK + (t + 1) * 2)
            e.dma_start(out=spill_v[t * 128:(t + 1) * 128, :],
                        in_=vstg[:, t % 2, :]).then_inc(qs.h, 16)
            qs.v += 16
    blk.sync(p3_vspill)
    V_SPILL = qs.v

    ve_prep0 = ve.v

    def p3_gs(e):
        if P3_LVL < 1:
            return
        for j in range(FC):
            if j >= 1:
                e.wait_ge(ve.h, ve_prep0 + j * 4)
            for g in range(NTG):
                kk = NQK + NV + j * NTG + g + 1
                e.wait_ge(te.h, te3 + kk)
                e.activation(prep_a[:, g * TGW:(g + 1) * TGW],
                             bank((kk - 1) % 4)[:, 0:TGW],
                             Act.Sigmoid).then_inc(se.h, 1)
                se.v += 1
            if NO_INPLACE:
                e.activation(fscA[:], prep_a[:], Act.Ln)
                e.activation(prep_a[:], fscA[:], Act.Copy).then_inc(se.h, 1)
            else:
                e.activation(prep_a[:], prep_a[:], Act.Ln).then_inc(se.h, 1)
            se.v += 1
            e.wait_ge(ve.h, ve_prep0 + j * 4 + 2)
            if NO_INPLACE:
                e.activation(fscA[:], prep_a[:], Act.Exp)
                e.activation(prep_a[:], fscA[:], Act.Copy).then_inc(se.h, 1)
            else:
                e.activation(prep_a[:], prep_a[:], Act.Exp).then_inc(se.h, 1)
            se.v += 1
            e.activation(ec_sb[:, j, :], lcc_sb[:, j, :],
                         Act.Exp).then_inc(se.h, 1)
            se.v += 1
    blk.scalar(p3_gs)

    def p3_prep_v(e):
        if P3_LVL < 1:
            return
        for j in range(FC):
            e.wait_ge(se.h, se3 + j * SPJ + NTG + 1)
            if NO_SCAN:
                e.tensor_copy(prep_c[:], prep_a[:])
            else:
                e.tensor_tensor_scan(prep_c[:], scanm_sb[:], prep_a[:], 0.0,
                                     Alu.mult, Alu.add)
            e.drain()
            e.tensor_copy(lcc_sb[:, j, :], prep_c[:, CH - 1::CH]).then_inc(ve.h, 1)
            ve.v += 1
            for c2 in range(NCH):
                ins = e.tensor_scalar(prep_a[:, c2 * CH:(c2 + 1) * CH],
                                      prep_c[:, c2 * CH:(c2 + 1) * CH],
                                      prep_c[:, (c2 + 1) * CH - 1:
                                              (c2 + 1) * CH],
                                      None, Alu.subtract)
            ins.then_inc(ve.h, 1)
            ve.v += 1
            e.wait_ge(se.h, se3 + j * SPJ + NTG + 2)
            e.tensor_tensor(qf[:, j, :], qf[:, j, :], prep_a[:],
                            Alu.mult).then_inc(ve.h, 1)
            ve.v += 1
            e.reciprocal(prep_a[:], prep_a[:])
            e.tensor_tensor(kf[:, j, :], kf[:, j, :], prep_a[:],
                            Alu.mult).then_inc(ve.h, 1)
            ve.v += 1
    blk.vector(p3_prep_v)
    VE_PREP = ve.v

    qs3 = qs.v

    def p3_ktr(e):
        if NO_KTR or P3_LVL < 2:
            return
        for j in range(FC):
            e.wait_ge(ve.h, ve_prep0 + (j + 1) * 4)
            for t in range(TC):
                e.dma_start_transpose(
                    ktok[:, t, j * 128:(j + 1) * 128],
                    kf[:, j, t * 128:(t + 1) * 128]).then_inc(qs.h, 16)
            qs.v += TC * 16
    blk.sync(p3_ktr)
    KTOK_DONE = qs.v

    def p3_xq(e):
        e.wait_ge(te.h, te_g_done if P3_LVL >= 1 else te_v_done)
        e.wait_ge(qs.h, V_SPILL)
        for j in range(FC):
            e.tensor_scalar(fxq[:], xn[:, j, :], srA_sb[:, 1:2], MAGIC,
                            Alu.mult, Alu.add)
            e.tensor_scalar(xn[:, j, :], fxq[:], -MAGIC, srA_sb[:, 0:1],
                            Alu.add, Alu.mult).then_inc(ve.h, 1)
            ve.v += 1
    blk.vector(p3_xq)

    barrier()
    if dbg and "d3" in dbg:
        def d3(e):
            for j in range(FC):
                e.dma_start(out=dbg_t["dbg_xq"][j * 128:(j + 1) * 128, :],
                            in_=xn[:, j, :]).then_inc(qs.h, 16)
                e.dma_start(out=dbg_t["dbg_qf"][j * 128:(j + 1) * 128, :],
                            in_=qf[:, j, :]).then_inc(qs.h, 16)
                e.dma_start(out=dbg_t["dbg_kf"][j * 128:(j + 1) * 128, :],
                            in_=kf[:, j, :]).then_inc(qs.h, 16)
                qs.v += 48
            e.dma_start(out=dbg_t["dbg_ec"],
                        in_=ec_sb[:].rearrange("p f w -> p (f w)")).then_inc(qs.h, 16)
            qs.v += 16
            e.wait_ge(qs.h, qs.v)
        blk.sync(d3)
        barrier()
    cut(3)

    qs5 = qs.v
    te5 = te.v
    se5 = se.v
    ve5 = ve.v
    qv5 = qv.v

    def p5_wload(e):
        e.wait_ge(cc.h, CC_W)
        for w in range(2):
            src = agup_o if w == 0 else aggt_o
            for icx in range(IC):
                kk = w * IC + icx
                if kk >= 2:
                    e.wait_ge(te.h, te5 + (kk - 1) * NTG)
                for jin in range(FC):
                    e.dma_start(out=ugw[:, kk % 2, jin, :],
                                in_=src[jin * 128:(jin + 1) * 128,
                                        icx * 128:(icx + 1) * 128]).then_inc(qs.h, 16)
                qs.v += FC * 16
    blk.sync(p5_wload)

    def p5_mm(e):
        for kk in range(2 * IC):
            e.wait_ge(qs.h, qs5 + (kk + 1) * FC * 16)
            if kk >= 1:
                e.wait_ge(se.h, se5 + kk * NTG)
            for jin in range(FC):
                for g in range(NTG):
                    ins = e.matmul(bank(g % 4)[:, 0:TGW],
                                   ugw[:, kk % 2, jin, :],
                                   xn[:, jin, g * TGW:(g + 1) * TGW],
                                   start=(jin == 0), stop=(jin == FC - 1))
                    if jin == FC - 1:
                        ins.then_inc(te.h, 1)
                        te.v += 1
    blk.tensor(p5_mm)

    qc5 = qc.v

    def p5_evac(e):
        for kk in range(2 * IC):
            if kk >= 2:
                e.wait_ge(qc.h, qc5 + (kk - 1) * 16)
                e.wait_ge(ve.h, ve5 + (kk - 2) + 1)
            for g in range(NTG):
                e.wait_ge(te.h, te5 + kk * NTG + g + 1)
                e.activation(ustg[:, kk % 2, g * TGW:(g + 1) * TGW],
                             bank(g % 4)[:, 0:TGW], Act.Copy).then_inc(se.h, 1)
                se.v += 1
            dst = spill_up if kk < IC else spill_gt
            icx = kk % IC
            e.wait_ge(se.h, se5 + (kk + 1) * NTG)
            e.dma_start(out=dst[icx * 128:(icx + 1) * 128, :],
                        in_=ustg[:, kk % 2, :]).then_inc(qc.h, 16)
            qc.v += 16
    blk.scalar(p5_evac)

    def p5_vec(e):
        for kk in range(2 * IC):
            e.wait_ge(se.h, se5 + (kk + 1) * NTG)
            col = 1 if kk < IC else 2
            e.reduce_max(stat_sb[:, 8:9], ustg[:, kk % 2, :], axis=AX,
                         apply_absolute_value=True)
            e.drain()
            if kk % IC == 0:
                e.tensor_copy(macc_sb[:, col:col + 1],
                              stat_sb[:, 8:9]).then_inc(ve.h, 1)
            else:
                e.tensor_tensor(macc_sb[:, col:col + 1],
                                macc_sb[:, col:col + 1], stat_sb[:, 8:9],
                                Alu.max).then_inc(ve.h, 1)
            ve.v += 1
    blk.vector(p5_vec)
    UG_SPILL = qc.v
    VE_UGMAX = ve.v

    def p5_bt(e):
        e.wait_ge(ve.h, VE_UGMAX)
        e.transpose(ps[0:2, 7, 0:128].bitcast(F32), macc_sb[:, 1:3],
                    eye_f_sb[:]).then_inc(te.h, 1)
        te.v += 1
    blk.tensor(p5_bt)
    B_TR = te.v

    def p5_bv(e):
        e.wait_ge(te.h, B_TR)
        e.reduce_max(m2_sb[0:2, 0:1], ps[0:2, 7, 0:128], axis=AX).then_inc(ve.h, 1)
        ve.v += 1
    blk.vector(p5_bv)
    B_SC = ve.v

    barrier()
    cut(4)

    te6 = te.v
    ve6 = ve.v
    qc6 = qc.v

    def p6_vload(e):
        e.wait_ge(qs.h, V_SPILL)
        for c in range(NCH):
            if c >= 2:
                e.wait_ge(te.h, te6 + (c - 2) * 3 + 3)
            sl2 = c % 2
            e.dma_start(out=vbuf_sb[0:64, sl2, :],
                        in_=spill_v[c * 64:(c + 1) * 64, :]).then_inc(qc.h, 16)
            e.dma_start(out=vbuf_sb[64:128, sl2, :],
                        in_=spill_v[c * 64:(c + 1) * 64, :]).then_inc(qc.h, 16)
            qc.v += 32
    blk.scalar(p6_vload)

    def p6_hinit(e):
        e.wait_ge(qc.h, UG_SPILL)
        e.memset(zrow_sb[:], 0.0)
        e.memset(h_sb[:], 0.0).then_inc(ve.h, 1)
        ve.v += 1
    blk.vector(p6_hinit)
    ve6b = ve.v

    def sbanks(par):
        return (4, 5) if par == 0 else (1, 3)

    def p6_mm(e):
        e.wait_ge(qs.h, KTOK_DONE)
        e.wait_ge(ve.h, ve6b)
        for c in range(NCH):
            par = c % 2
            e.wait_ge(qc.h, qc6 + (c + 1) * 32)
            if c >= 2:
                e.wait_ge(ve.h, ve6b + (c - 2) * 4 + 4)
            sb0, sb1 = sbanks(par)
            for p in range(FC):
                for hh in range(2):
                    rb = 64 * hh
                    sp = bank(sb0 if hh == 0 else sb1)
                    ins = e.matmul(
                        sp[rb:rb + 64, p * 64:(p + 1) * 64],
                        kf[rb:rb + 64, p, c * 64:(c + 1) * 64],
                        qf[rb:rb + 64, p, c * 64:(c + 1) * 64],
                        start=True, stop=True, tile_position=(rb, rb),
                        skip_group_check=True)
            ins.then_inc(te.h, 1)
            te.v += 1
            kvb = bank(6 if par == 0 else 7)
            for p in range(FC):
                for hh in range(2):
                    ins = e.matmul(
                        kvb[hh * 64:(hh + 1) * 64, p * 64:(p + 1) * 64],
                        ktok[par * 64:(par + 1) * 64, c // 2,
                             (2 * p + hh) * 64:(2 * p + hh + 1) * 64],
                        vbuf_sb[par * 64:(par + 1) * 64, par,
                                (2 * p + hh) * 64:(2 * p + hh + 1) * 64],
                        start=True, stop=True, tile_position=(par * 64, hh * 64),
                        skip_group_check=True)
            ins.then_inc(te.h, 1)
            te.v += 1
            e.wait_ge(ve.h, ve6b + c * 4 + 2)
            yb = bank(0 if par == 0 else 2)
            for p in range(FC):
                for hh in range(2):
                    rb = 64 * hh
                    e.matmul(yb[rb:rb + 64, p * 64:(p + 1) * 64],
                             vbuf_sb[rb:rb + 64, par,
                                     (2 * p + hh) * 64:(2 * p + hh + 1) * 64],
                             sm_sb[rb:rb + 64, hh, p * 64:(p + 1) * 64],
                             start=True, stop=False,
                             tile_position=(rb, rb),
                             skip_group_check=True)
                    ins = e.matmul(
                        yb[rb:rb + 64, p * 64:(p + 1) * 64],
                        g0_sb[rb:rb + 64, p, :],
                        qf[rb:rb + 64, p, c * 64:(c + 1) * 64],
                        start=False, stop=True, tile_position=(rb, rb),
                        skip_group_check=True)
            ins.then_inc(te.h, 1)
            te.v += 1
    blk.tensor(p6_mm)
    TE_LOOP_END = te.v

    def p6_v(e):
        for c in range(NCH):
            par = c % 2
            sb0, sb1 = sbanks(par)
            e.wait_ge(te.h, te6 + c * 3 + 1)
            for hh in range(2):
                rb = 64 * hh
                sp = bank(sb0 if hh == 0 else sb1)
                ins = e.tensor_tensor(
                    sm_sb[rb:rb + 64, hh, :],
                    sp[rb:rb + 64, :],
                    tril_sb[rb:rb + 64, :], Alu.mult)
            ins.then_inc(ve.h, 1)
            ve.v += 1
            for f2 in range(FC):
                ins = e.tensor_scalar(g0_sb[:, f2, :], h_sb[:, f2, :],
                                      ec_sb[:, f2, c:c + 1], None, Alu.mult)
            ins.then_inc(ve.h, 1)
            ve.v += 1
            if c >= 1:
                e.wait_ge(te.h, te6 + (c - 1) * 3 + 3)
                pprev = (c - 1) % 2
                e.tensor_copy(yf[:, :, (c - 1) * 64:c * 64],
                              bank(0 if pprev == 0 else 2).rearrange(
                                  "p (f w) -> p f w", w=64)).then_inc(ve.h, 1)
            else:
                e.tensor_copy(stat_sb[:, 11:12],
                              stat_sb[:, 8:9]).then_inc(ve.h, 1)
            ve.v += 1
            e.wait_ge(te.h, te6 + c * 3 + 2)
            e.drain()
            kvb = bank(6 if par == 0 else 7)
            e.tensor_tensor(h_sb[:], g0_sb[:],
                            kvb.rearrange("p (f w) -> p f w", w=HD),
                            Alu.add).then_inc(ve.h, 1)
            ve.v += 1
        e.wait_ge(te.h, te6 + (NCH - 1) * 3 + 3)
        pprev = (NCH - 1) % 2
        e.tensor_copy(yf[:, :, (NCH - 1) * 64:NCH * 64],
                      bank(0 if pprev == 0 else 2).rearrange(
                          "p (f w) -> p f w", w=64)).then_inc(ve.h, 1)
        ve.v += 1
    blk.vector(p6_v)
    VE_LOOP_END = ve.v

    def p6_hag(e):
        e.wait_ge(ve.h, VE_LOOP_END)
        e.dma_start(out=agh_i[0:128, :],
                    in_=h_sb[:].rearrange("p f w -> p (f w)")).then_inc(qg.h, 16)
        qg.v += 16
        e.dma_start(out=agh_i[128:129, :], in_=zrow_sb[:]).then_inc(qg.h, 16)
        qg.v += 16
        e.wait_ge(qg.h, qg.v)
        e.dma_start(out=agh_i[128:129, 0:2],
                    in_=m2_sb[0:2, 0:1]).then_inc(qg.h, 16)
        qg.v += 16
        e.wait_ge(qg.h, qg.v)
        e.collective_compute("AllGather", Alu.bypass, replica_groups=RG,
                             ins=[agh_i.opt()], outs=[agh_o.opt()]).then_inc(cc.h, 1)
        cc.v += 1
    blk.gpsimd(p6_hag)
    CC_H = cc.v

    barrier()
    if dbg and "d6" in dbg:
        def d6(e):
            for j in range(FC):
                e.dma_start(out=dbg_t["dbg_y"][j * 128:(j + 1) * 128, :],
                            in_=yf[:, j, :]).then_inc(qs.h, 16)
                qs.v += 16
            e.dma_start(out=dbg_t["dbg_h"],
                        in_=h_sb[:].rearrange("p f w -> p (f w)")).then_inc(qs.h, 16)
            qs.v += 16
            e.wait_ge(qs.h, qs.v)
        blk.sync(d6)
        barrier()

    qs7 = qs.v
    ve7 = ve.v
    te7 = te.v
    se7 = se.v
    B_FIN = ve7 + 1
    SRB_CP = ve7 + 2
    HACC0 = ve7 + 2
    HIN_BF = ve7 + 2 + NC + 1
    E2V0 = HIN_BF
    E2V_END = E2V0 + FC
    CORR0 = E2V_END
    GP0 = CORR0 + 2 * NCH
    GP_END = GP0 + 3 * IC
    C_SC = GP_END + 1
    C_FIN = C_SC + 1
    SRC_DONE = C_FIN + 1
    B_BC = te7 + 1
    CORR_T0 = te7 + 1
    C_TR = te7 + 1 + NCH + 1
    C_BC = C_TR + 1
    E2_S = se7 + 1
    G2GRL0 = qs.v + 16 + NC * 16

    def p7_sync(e):
        e.wait_ge(cc.h, CC_H)
        mm = agh_o[128::129, 0:2].transpose([1, 0])
        e.dma_start(out=mfin_sb[0:1, 16:16 + 2 * NC].rearrange(
            "p (a b) -> p a b", b=NC), in_=mm,
            allow_slow_non_contiguous=True).then_inc(qs.h, 16)
        qs.v += 16
        for r in range(NC):
            if r >= 2:
                e.wait_ge(ve.h, HACC0 + r - 1)
            e.dma_start(out=hld_sb[:, r % 2, :],
                        in_=agh_o[r * 129:r * 129 + 128, :]).then_inc(qs.h, 16)
            qs.v += 16
        e.wait_ge(qc.h, UG_SPILL)
        for icx in range(IC):
            if icx >= 2:
                e.wait_ge(ve.h, GP0 + (icx - 2) * 3 + 3)
            e.dma_start(out=grl[:, icx % 2, :],
                        in_=spill_gt[icx * 128:(icx + 1) * 128, :]).then_inc(qs.h, 16)
            qs.v += 16
            if icx >= 2:
                e.wait_ge(ve.h, GP0 + (icx - 2) * 3 + 2)
                e.dma_start(out=spill_g2[(icx - 2) * 128:(icx - 1) * 128, :],
                            in_=g2st[:, icx % 2, :]).then_inc(qs.h, 16)
                qs.v += 16
        for icx in range(IC - 2, IC):
            e.wait_ge(ve.h, GP0 + icx * 3 + 2)
            e.dma_start(out=spill_g2[icx * 128:(icx + 1) * 128, :],
                        in_=g2st[:, icx % 2, :]).then_inc(qs.h, 16)
            qs.v += 16
    blk.sync(p7_sync)
    G2SP_END = qs.v

    def p7_scalar(e):
        e.wait_ge(ve.h, E2V_END)
        e.activation(e2_sb[:].rearrange("p f w -> p (f w)"),
                     e2_sb[:].rearrange("p f w -> p (f w)"),
                     Act.Exp).then_inc(se.h, 1)
        se.v += 1
        for icx in range(IC):
            e.wait_ge(ve.h, GP0 + icx * 3 + 1)
            e.activation(fscB[:], gq_bf[:], Act.Sigmoid)
            e.activation(fscB[:], fscB[:], Act.Copy, bias=MAGIC,
                         scale=65536.0).then_inc(se.h, 1)
            se.v += 1
    blk.scalar(p7_scalar)

    def p7_tensor(e):
        e.wait_ge(ve.h, B_FIN)
        e.matmul(ps[:, 7, 0:4].bitcast(F32), of1x_sb[:], mfin_sb[0:1, 10:14],
                 start=True, stop=True).then_inc(te.h, 1)
        te.v += 1
        for c in range(NCH):
            e.wait_ge(ve.h, CORR0 + c * 2 + 1)
            if c >= 2:
                e.wait_ge(ve.h, CORR0 + (c - 2) * 2 + 2)
            cb = bank(c % 2)
            for p in range(FC):
                for hh in range(2):
                    rb = hh * 64
                    e.matmul(cb[rb:rb + 64, p * 64:(p + 1) * 64],
                             hin_bf[rb:rb + 64, p, :],
                             qhat_sb[rb:rb + 64, p, :],
                             start=True, stop=True,
                             tile_position=(rb, rb),
                             skip_group_check=True)
            e.drain()
            e.nop().then_inc(te.h, 1)
            te.v += 1
        e.wait_ge(ve.h, GP_END)
        e.transpose(ps[0:1, 6, 0:128].bitcast(F32), macc_sb[:, 3:4],
                    eye_f_sb[:]).then_inc(te.h, 1)
        te.v += 1
        e.wait_ge(ve.h, C_FIN)
        e.matmul(ps[:, 7, 0:2].bitcast(F32), of1x_sb[:], mfin_sb[0:1, 21:23],
                 start=True, stop=True).then_inc(te.h, 1)
        te.v += 1
    blk.tensor(p7_tensor)

    def p7_vector(e):
        e.wait_ge(qs.h, qs7 + 16)
        e.memset(hin_sb[:], 0.0)
        e.drain()
        e.reduce_max(mfin_sb[0:1, 4:6],
                     mfin_sb[0:1, 16:16 + 2 * NC].rearrange(
                         "p (a b) -> p a b", b=NC), axis=AX)
        e.drain()
        e.tensor_scalar(mfin_sb[0:1, 6:8], mfin_sb[0:1, 4:6], 1.0 / QMAX,
                        1e-12, Alu.mult, Alu.max)
        e.drain()
        e.reciprocal(mfin_sb[0:1, 8:10], mfin_sb[0:1, 6:8])
        e.drain()
        e.tensor_copy(mfin_sb[0:1, 10:11], mfin_sb[0:1, 6:7])
        e.tensor_copy(mfin_sb[0:1, 11:12], mfin_sb[0:1, 8:9])
        e.tensor_copy(mfin_sb[0:1, 12:13], mfin_sb[0:1, 7:8])
        e.tensor_copy(mfin_sb[0:1, 13:14], mfin_sb[0:1, 9:10]).then_inc(ve.h, 1)
        ve.v += 1
        e.wait_ge(te.h, B_BC)
        e.tensor_copy(srB_sb[:], ps[:, 7, 0:4]).then_inc(ve.h, 1)
        ve.v += 1
        for r in range(NC):
            e.wait_ge(qs.h, qs7 + 16 + (r + 1) * 16)
            e.scalar_tensor_tensor(hin_sb[:].rearrange("p f w -> p (f w)"),
                                   hld_sb[:, r % 2, :], selH_sb[:, r:r + 1],
                                   hin_sb[:].rearrange("p f w -> p (f w)"),
                                   Alu.mult, Alu.add).then_inc(ve.h, 1)
            ve.v += 1
        e.drain()
        e.tensor_copy(hin_bf[:], hin_sb[:]).then_inc(ve.h, 1)
        ve.v += 1
        for j in range(FC):
            e.tensor_tensor_scan(e2_sb[:, j, :], onesch_sb[:], lcc_sb[:, j, :],
                                 0.0, Alu.mult, Alu.add).then_inc(ve.h, 1)
            ve.v += 1
        e.wait_ge(se.h, E2_S)
        for c in range(NCH):
            if c >= 1:
                e.wait_ge(te.h, CORR_T0 + c)
            for f2 in range(FC):
                ins = e.tensor_scalar(qhat_sb[:, f2, :],
                                      qf[:, f2, c * 64:(c + 1) * 64],
                                      e2_sb[:, f2, c:c + 1], None, Alu.mult)
            ins.then_inc(ve.h, 1)
            ve.v += 1
            e.wait_ge(te.h, CORR_T0 + c + 1)
            if DLY_YADD:
                e.nop(cycle_cnt=DLY_YADD)
            e.tensor_tensor(yf[:, :, c * 64:(c + 1) * 64],
                            yf[:, :, c * 64:(c + 1) * 64],
                            bank(c % 2).rearrange("p (f w) -> p f w", w=64),
                            Alu.add).then_inc(ve.h, 1)
            ve.v += 1
        for icx in range(IC):
            e.wait_ge(qs.h, qs7 + 16 + NC * 16 + (icx + 1) * 16)
            e.tensor_scalar(fscA[:], grl[:, icx % 2, :], srB_sb[:, 3:4],
                            MAGIC, Alu.mult, Alu.add)
            e.tensor_scalar(gq_bf[:], fscA[:], -MAGIC, srB_sb[:, 2:3],
                            Alu.add, Alu.mult).then_inc(ve.h, 1)
            ve.v += 1
            e.wait_ge(se.h, E2_S + icx + 1)
            if icx >= 2:
                e.wait_ge(qs.h, G2GRL0 + icx * 16 + (icx - 1) * 16)
            e.tensor_scalar(fscB[:], fscB[:], -MAGIC, 65535.0, Alu.add,
                            Alu.min)
            e.tensor_tensor(g2st[:, icx % 2, :], gq_bf[:], fscB[:],
                            Alu.mult).then_inc(ve.h, 1)
            ve.v += 1
            e.reduce_max(stat_sb[:, 9:10], g2st[:, icx % 2, :], axis=AX,
                         apply_absolute_value=True)
            e.drain()
            if icx == 0:
                e.tensor_copy(macc_sb[:, 3:4],
                              stat_sb[:, 9:10]).then_inc(ve.h, 1)
            else:
                e.tensor_tensor(macc_sb[:, 3:4], macc_sb[:, 3:4],
                                stat_sb[:, 9:10], Alu.max).then_inc(ve.h, 1)
            ve.v += 1
    blk.vector(p7_vector)
    assert ve.v == GP_END, (ve.v, GP_END)
    assert te.v == C_BC, (te.v, C_BC)

    def p7_cv(e):
        e.wait_ge(te.h, C_TR)
        e.reduce_max(mfin_sb[0:1, 20:21], ps[0:1, 6, 0:128],
                     axis=AX).then_inc(ve.h, 1)
        ve.v += 1
    blk.vector(p7_cv)

    def p7_cag(e):
        e.wait_ge(ve.h, C_SC)
        e.dma_start(out=agc_i, in_=mfin_sb[0:1, 20:21]).then_inc(qg.h, 16)
        qg.v += 16
        e.wait_ge(qg.h, qg.v)
        e.collective_compute("AllGather", Alu.bypass, replica_groups=RG,
                             ins=[agc_i.opt()], outs=[agc_o.opt()]).then_inc(cc.h, 1)
        cc.v += 1
        e.wait_ge(cc.h, cc.v)
        e.dma_start(out=mfin_sb[0:1, 24:24 + NC], in_=agc_o.opt()).then_inc(qg.h, 16)
        qg.v += 16
    blk.gpsimd(p7_cag)
    CC_C = cc.v
    AGC_LD = qg.v

    def p7_cfin(e):
        e.wait_ge(qg.h, AGC_LD)
        e.reduce_max(mfin_sb[0:1, 20:21], mfin_sb[0:1, 24:24 + NC], axis=AX)
        e.drain()
        e.tensor_scalar(mfin_sb[0:1, 21:22], mfin_sb[0:1, 20:21],
                        1.0 / (65536.0 * QMAX), 1e-12, Alu.mult, Alu.max)
        e.drain()
        e.reciprocal(mfin_sb[0:1, 22:23], mfin_sb[0:1, 21:22])
        e.drain()
        e.tensor_scalar(mfin_sb[0:1, 22:23], mfin_sb[0:1, 22:23],
                        1.0 / 65536.0, None, Alu.mult).then_inc(ve.h, 1)
        ve.v += 1
    blk.vector(p7_cfin)

    def p7_cbv(e):
        e.wait_ge(te.h, C_BC)
        e.tensor_copy(srC_sb[:], ps[:, 7, 0:2]).then_inc(ve.h, 1)
        ve.v += 1
    blk.vector(p7_cbv)
    assert ve.v == SRC_DONE, (ve.v, SRC_DONE)

    barrier()
    if dbg and "d7y" in dbg:
        def d7y(e):
            for j in range(FC):
                e.dma_start(out=dbg_t["dbg_y7"][j * 128:(j + 1) * 128, :],
                            in_=yf[:, j, :]).then_inc(qs.h, 16)
                qs.v += 16
            e.wait_ge(qs.h, qs.v)
        blk.sync(d7y)
        barrier()
    cut(5)
    if dbg and "d7w" in dbg:
        def d7w(e):
            e.wait_ge(cc.h, CC_W)
            e.dma_start(out=dbg_t["dbg_wup"], in_=agup_o).then_inc(qc.h, 16)
            e.dma_start(out=dbg_t["dbg_wdn"], in_=agdn_o).then_inc(qc.h, 16)
            e.dma_start(out=dbg_t["dbg_up"], in_=spill_up).then_inc(qc.h, 16)
            e.dma_start(out=dbg_t["dbg_gt"], in_=spill_gt).then_inc(qc.h, 16)
            e.dma_start(out=dbg_t["dbg_g2"], in_=spill_g2).then_inc(qc.h, 16)
            qc.v += 80
            e.wait_ge(qc.h, qc.v)
        blk.scalar(d7w)

        def d7(e):
            e.dma_start(out=dbg_t["dbg_sr"][:, 0:2], in_=srA_sb[:]).then_inc(qs.h, 16)
            e.dma_start(out=dbg_t["dbg_sr"][:, 2:6], in_=srB_sb[:]).then_inc(qs.h, 16)
            e.dma_start(out=dbg_t["dbg_sr"][:, 6:8], in_=srC_sb[:]).then_inc(qs.h, 16)
            qs.v += 48
            e.wait_ge(qs.h, qs.v)
        blk.sync(d7)
        barrier()

    qg8 = qg.v

    def p8_wo(e):
        for j in range(FC):
            e.dma_start(out=wo_bf[:, j, :],
                        in_=wo_t[j * 128:(j + 1) * 128, :]).then_inc(qg.h, 16)
        qg.v += FC * 16
    blk.gpsimd(p8_wo)
    WO_LD = qg.v

    se8 = se.v
    te8 = te.v
    ve8 = ve.v

    def p8_scalar(e):
        for g in range(NTG):
            if g >= 1:
                e.wait_ge(te.h, te8 + 2 * g - 1)
            e.activation(y2s[:], yf[:, :, g * TGW:(g + 1) * TGW],
                         Act.Square).then_inc(se.h, 1)
            se.v += 1
            e.wait_ge(ve.h, ve8 + g * 3 + 1)
            e.sqrt(nw_g[0:1, 0:TGW], vv_g[0:1, 0:TGW]).then_inc(se.h, 1)
            se.v += 1
    blk.scalar(p8_scalar)

    def p8_stat_t(e):
        for g in range(NTG):
            e.wait_ge(se.h, se8 + g * 2 + 1)
            if g >= 2:
                e.wait_ge(ve.h, ve8 + (g - 2) * 3 + 1)
            for j in range(FC):
                e.matmul(ps[0:1, 4 + g % 2, 0:TGW], obfc1_sb[:],
                         yf[:, j, g * TGW:(g + 1) * TGW],
                         start=(j == 0), stop=(j == FC - 1),
                         skip_group_check=True)
                ins = e.matmul(ps[0:1, 6 + g % 2, 0:TGW], obfc1_sb[:],
                               y2s[:, j, 0:TGW],
                               start=(j == 0), stop=(j == FC - 1),
                               skip_group_check=True)
                if j == FC - 1:
                    e.drain()
                    e.nop().then_inc(te.h, 1)
                    te.v += 1
            e.wait_ge(ve.h, ve8 + g * 3 + 2)
            if g >= 1:
                e.wait_ge(ve.h, ve8 + (g - 1) * 3 + 3)
            e.matmul(ps[:, 2, 0:TGW].bitcast(F32), of1x_sb[:],
                     rl_g[0:1, 0:TGW], start=True, stop=True,
                     skip_group_check=True)
            e.matmul(ps[:, 3, 0:TGW].bitcast(F32), of1x_sb[:],
                     mu_g[0:1, 0:TGW], start=True, stop=True,
                     skip_group_check=True)
            e.drain()
            e.nop().then_inc(te.h, 1)
            te.v += 1
    blk.tensor(p8_stat_t)

    def p8_vector(e):
        for g in range(NTG):
            e.wait_ge(te.h, te8 + 2 * g + 1)
            if DLY_STAT:
                e.nop(cycle_cnt=DLY_STAT)
            e.tensor_scalar(mu_g[0:1, 0:TGW], ps[0:1, 4 + g % 2, 0:TGW],
                            1.0 / DM, None, Alu.mult)
            e.tensor_scalar(vv_g[0:1, 0:TGW], ps[0:1, 6 + g % 2, 0:TGW],
                            1.0 / DM, None, Alu.mult)
            e.drain()
            e.tensor_tensor(rl_g[0:1, 0:TGW], mu_g[0:1, 0:TGW],
                            mu_g[0:1, 0:TGW], Alu.mult)
            e.drain()
            e.tensor_tensor(vv_g[0:1, 0:TGW], vv_g[0:1, 0:TGW],
                            rl_g[0:1, 0:TGW], Alu.subtract)
            e.drain()
            e.tensor_scalar(vv_g[0:1, 0:TGW], vv_g[0:1, 0:TGW], LN_EPS, None,
                            Alu.add).then_inc(ve.h, 1)
            ve.v += 1
            e.wait_ge(se.h, se8 + g * 2 + 2)
            e.reciprocal(rl_g[0:1, 0:TGW], nw_g[0:1, 0:TGW])
            e.drain()
            e.tensor_tensor(nw_g[0:1, 0:TGW], rl_g[0:1, 0:TGW],
                            rl_g[0:1, 0:TGW], Alu.mult)
            e.drain()
            e.tensor_tensor(nw_g[0:1, 0:TGW], nw_g[0:1, 0:TGW],
                            vv_g[0:1, 0:TGW], Alu.mult)
            e.drain()
            e.tensor_scalar(nw_g[0:1, 0:TGW], nw_g[0:1, 0:TGW], -0.5, 1.5,
                            Alu.mult, Alu.add)
            e.drain()
            e.tensor_tensor(rl_g[0:1, 0:TGW], rl_g[0:1, 0:TGW],
                            nw_g[0:1, 0:TGW], Alu.mult)
            e.drain()
            e.tensor_tensor(mu_g[0:1, 0:TGW], mu_g[0:1, 0:TGW],
                            rl_g[0:1, 0:TGW], Alu.mult).then_inc(ve.h, 1)
            ve.v += 1
            e.wait_ge(te.h, te8 + 2 * g + 2)
            if DLY_RESC:
                e.nop(cycle_cnt=DLY_RESC)
            for j2 in range(FC):
                y2d = yf[:, j2, g * TGW:(g + 1) * TGW]
                e.tensor_tensor(y2d, y2d, ps[:, 2, 0:TGW], Alu.mult)
                ins = e.tensor_tensor(y2d, y2d, ps[:, 3, 0:TGW], Alu.subtract)
            ins.then_inc(ve.h, 1)
            ve.v += 1
    blk.vector(p8_vector)
    YLN_DONE = ve.v

    barrier()
    cut(6)
    if dbg and "d8" in dbg:
        def d8(e):
            for j in range(FC):
                e.dma_start(out=dbg_t["dbg_yl"][j * 128:(j + 1) * 128, :],
                            in_=yf[:, j, :]).then_inc(qs.h, 16)
                qs.v += 16
            e.wait_ge(qs.h, qs.v)
        blk.sync(d8)
        barrier()

    te9 = te.v
    ve9 = ve.v
    se9 = se.v

    def p9_v0(e):
        e.wait_ge(qg.h, WO_LD)
        e.tensor_copy(beta_bf[:], beta_sb[:]).then_inc(ve.h, 1)
        ve.v += 1
    blk.vector(p9_v0)
    BB_DONE = ve.v

    def p9_bo_t(e):
        e.wait_ge(ve.h, BB_DONE)
        for eg in range(2):
            for j in range(FC):
                ins = e.matmul(ps[0:1, 6 + eg, 0:512], beta_bf[:, j:j + 1],
                               wo_bf[:, j, eg * 512:(eg + 1) * 512],
                               start=(j == 0), stop=(j == FC - 1),
                               skip_group_check=True)
            ins.then_inc(te.h, 1)
            te.v += 1
    blk.tensor(p9_bo_t)
    BO_T = te.v

    def p9_v1(e):
        for eg in range(2):
            e.wait_ge(te.h, te9 + 1 + eg)
            e.tensor_copy(bo_sb[0:1, eg * 512:(eg + 1) * 512],
                          ps[0:1, 6 + eg, 0:512]).then_inc(ve.h, 1)
            ve.v += 1
        for j in range(FC):
            e.tensor_scalar(wo_bf[:, j, :], wo_bf[:, j, :],
                            gamma_sb[:, j:j + 1], None,
                            Alu.mult).then_inc(ve.h, 1)
            ve.v += 1
    blk.vector(p9_v1)
    WO_SCALED = ve.v

    def p9_mm(e):
        e.wait_ge(ve.h, WO_SCALED)
        k = 0
        for t in range(TC):
            for eg in range(2):
                k += 1
                b = (k - 1) % 4
                if k > 4:
                    e.wait_ge(se.h, se9 + (k - 4))
                for j in range(FC):
                    e.matmul(bank(b)[:],
                             yf[:, j, t * 128:(t + 1) * 128],
                             wo_bf[:, j, eg * 512:(eg + 1) * 512],
                             start=(j == 0), stop=False,
                             skip_group_check=True)
                e.matmul(bank(b)[:], obf1x_sb[:],
                         bo_sb[0:1, eg * 512:(eg + 1) * 512],
                         start=False, stop=True,
                         skip_group_check=True).then_inc(te.h, 1)
                te.v += 1
    blk.tensor(p9_mm)
    WO_MM_END = te.v

    def p9_ev(e):
        k = 0
        for t in range(TC):
            for eg in range(2):
                k += 1
                e.wait_ge(te.h, BO_T + k)
                e.activation(stok[:, t, eg * 512:(eg + 1) * 512],
                             bank((k - 1) % 4)[:], Act.Copy).then_inc(se.h, 1)
                se.v += 1
    blk.scalar(p9_ev)
    STOK_DONE = se.v

    barrier()
    cut(7)
    if dbg and "mid10" in dbg:
        def dm10(e):
            for j in range(FC):
                e.dma_start(out=dbg_t["dbg_y7"][j * 128:(j + 1) * 128, :],
                            in_=yf[:, j, :]).then_inc(qs.h, 16)
                qs.v += 16
            e.wait_ge(qs.h, qs.v)
        blk.sync(dm10)
        barrier()

    qs10 = qs.v
    ve10 = ve.v

    def p10_sync(e):
        e.wait_ge(qs.h, G2SP_END)
        for icx in range(IC):
            if icx >= 2:
                e.wait_ge(ve.h, ve10 + (icx - 2) * 2 + 2)
            e.dma_start(out=url[:, icx % 2, :],
                        in_=spill_up[icx * 128:(icx + 1) * 128, :]).then_inc(qs.h, 16)
            e.dma_start(out=g2rl[:, icx % 2, :],
                        in_=spill_g2[icx * 128:(icx + 1) * 128, :]).then_inc(qs.h, 16)
            qs.v += 32
    blk.sync(p10_sync)

    def p10_v(e):
        for icx in range(IC):
            e.wait_ge(qs.h, qs10 + (icx + 1) * 32)
            e.tensor_scalar(foq[:], url[:, icx % 2, :], srB_sb[:, 1:2], MAGIC,
                            Alu.mult, Alu.add)
            e.tensor_scalar(uq_bf[:], foq[:], -MAGIC, srB_sb[:, 0:1],
                            Alu.add, Alu.mult)
            e.tensor_scalar(foq[:], g2rl[:, icx % 2, :], srC_sb[:, 1:2],
                            MAGIC, Alu.mult, Alu.add)
            e.tensor_scalar(g2q_bf[:], foq[:], -MAGIC, srC_sb[:, 0:1],
                            Alu.add, Alu.mult)
            e.tensor_tensor(o_all[:, icx, :], uq_bf[:], g2q_bf[:],
                            Alu.mult).then_inc(ve.h, 1)
            ve.v += 1
            e.reduce_max(stat_sb[:, 10:11], o_all[:, icx, :], axis=AX,
                         apply_absolute_value=True)
            e.drain()
            if icx == 0:
                e.tensor_copy(macc_sb[:, 4:5], stat_sb[:, 10:11]).then_inc(ve.h, 1)
            else:
                e.tensor_tensor(macc_sb[:, 4:5], macc_sb[:, 4:5],
                                stat_sb[:, 10:11], Alu.max).then_inc(ve.h, 1)
            ve.v += 1
    blk.vector(p10_v)
    OPROC_V = ve.v

    def p10_dt(e):
        e.wait_ge(ve.h, OPROC_V)
        e.transpose(ps[0:1, 7, 0:128].bitcast(F32), macc_sb[:, 4:5],
                    eye_f_sb[:]).then_inc(te.h, 1)
        te.v += 1
    blk.tensor(p10_dt)
    D_TR = te.v

    def p10_dv(e):
        e.wait_ge(te.h, D_TR)
        e.reduce_max(mfin_sb[0:1, 32:33], ps[0:1, 7, 0:128],
                     axis=AX).then_inc(ve.h, 1)
        ve.v += 1
    blk.vector(p10_dv)
    D_SC = ve.v

    def p10_dag(e):
        e.wait_ge(ve.h, D_SC)
        e.dma_start(out=agd_i, in_=mfin_sb[0:1, 32:33]).then_inc(qg.h, 16)
        qg.v += 16
        e.wait_ge(qg.h, qg.v)
        e.collective_compute("AllGather", Alu.bypass, replica_groups=RG,
                             ins=[agd_i.opt()], outs=[agd_o.opt()]).then_inc(cc.h, 1)
        cc.v += 1
        e.wait_ge(cc.h, cc.v)
        e.dma_start(out=mfin_sb[0:1, 36:36 + NC], in_=agd_o.opt()).then_inc(qg.h, 16)
        qg.v += 16
    blk.gpsimd(p10_dag)
    CC_D = cc.v
    AGD_LD = qg.v

    def p10_dfin(e):
        e.wait_ge(qg.h, AGD_LD)
        e.reduce_max(mfin_sb[0:1, 32:33], mfin_sb[0:1, 36:36 + NC], axis=AX)
        e.drain()
        e.tensor_scalar(mfin_sb[0:1, 33:34], mfin_sb[0:1, 32:33], 1.0 / QMAX,
                        1e-12, Alu.mult, Alu.max)
        e.drain()
        e.reciprocal(mfin_sb[0:1, 34:35], mfin_sb[0:1, 33:34]).then_inc(ve.h, 1)
        ve.v += 1
    blk.vector(p10_dfin)
    D_FIN = ve.v

    def p10_dbt(e):
        e.wait_ge(ve.h, D_FIN)
        e.matmul(ps[:, 7, 0:2].bitcast(F32), of1x_sb[:], mfin_sb[0:1, 33:35],
                 start=True, stop=True).then_inc(te.h, 1)
        te.v += 1
    blk.tensor(p10_dbt)
    D_BC = te.v

    def p10_oq(e):
        e.wait_ge(te.h, D_BC)
        e.tensor_copy(srD_sb[:], ps[:, 7, 0:2])
        e.drain()
        e.tensor_copy(stat_sb[:, 11:12], stat_sb[:, 8:9]).then_inc(ve.h, 1)
        ve.v += 1
        for icx in range(IC):
            e.tensor_scalar(foq[:], o_all[:, icx, :], srD_sb[:, 1:2], MAGIC,
                            Alu.mult, Alu.add)
            e.tensor_scalar(o_all[:, icx, :], foq[:], -MAGIC, srD_sb[:, 0:1],
                            Alu.add, Alu.mult).then_inc(ve.h, 1)
            ve.v += 1
    blk.vector(p10_oq)
    OQ_DONE = ve.v

    barrier()
    cut(8)

    qs11 = qs.v
    te11 = te.v
    se11 = se.v
    NTB = max(TC // 4, 1)
    TBW = TC // NTB

    def p11_wload(e):
        for tb in range(NTB):
            for icx in range(IC):
                kk = tb * IC + icx
                if kk >= 2:
                    e.wait_ge(te.h, te11 + kk - 1)
                e.dma_start(out=dnw[:, kk % 3, :],
                            in_=agdn_o[icx * 128:(icx + 1) * 128, :]).then_inc(qs.h, 16)
                qs.v += 16
    blk.sync(p11_wload)

    def p11_mm(e):
        if DLY_GTOK:
            e.nop(cycle_cnt=DLY_GTOK)
        for tb in range(NTB):
            if tb >= 1:
                e.wait_ge(se.h, se11 + tb * TBW * 2)
            for icx in range(IC):
                kk = tb * IC + icx
                e.wait_ge(qs.h, qs11 + (kk + 1) * 16)
                for tt in range(TBW):
                    t = tb * TBW + tt
                    for eg in range(2):
                        ins = e.matmul(ps[:, (tt * 2 + eg) % 8, :],
                                       o_all[:, icx, t * 128:(t + 1) * 128],
                                       dnw[:, kk % 3, eg * 512:(eg + 1) * 512],
                                       start=(icx == 0), stop=(icx == IC - 1),
                                       skip_group_check=True)
                ins.then_inc(te.h, 1)
                te.v += 1
    blk.tensor(p11_mm)

    def p11_ev(e):
        for tb in range(NTB):
            e.wait_ge(te.h, te11 + (tb + 1) * IC)
            if DLY_GTOK:
                e.nop(cycle_cnt=DLY_GTOK)
            for tt in range(TBW):
                t = tb * TBW + tt
                for eg in range(2):
                    e.activation(gtok[:, t, eg * 512:(eg + 1) * 512],
                                 ps[:, (tt * 2 + eg) % 8, :],
                                 Act.Silu).then_inc(se.h, 1)
                    se.v += 1
    blk.scalar(p11_ev)
    GTOK_DONE = se.v

    barrier()
    cut(9)

    qs12 = qs.v
    ve12 = ve.v
    OUT0 = qc.v

    def p12_xload(e):
        for t in range(TC):
            if t >= 2:
                e.wait_ge(ve.h, ve12 + (t - 2) * 2 + 2)
            e.dma_start(out=xrl[:, t % 2, :],
                        in_=x_in[t * 128:(t + 1) * 128, :]).then_inc(qs.h, 16)
            qs.v += 16
    blk.sync(p12_xload)

    def p12_v(e):
        for t in range(TC):
            e.wait_ge(qs.h, qs12 + (t + 1) * 16)
            if t >= 2:
                e.wait_ge(qc.h, OUT0 + (t - 1) * 16)
            e.tensor_tensor(sgt[:], stok[:, t, :], gtok[:, t, :],
                            Alu.mult).then_inc(ve.h, 1)
            ve.v += 1
            e.tensor_tensor(ost[:, t % 2, :], xrl[:, t % 2, :], sgt[:],
                            Alu.add).then_inc(ve.h, 1)
            ve.v += 1
    blk.vector(p12_v)

    def p12_st(e):
        for t in range(TC):
            e.wait_ge(ve.h, ve12 + t * 2 + 2)
            e.dma_start(out=out_d[t * 128:(t + 1) * 128, :],
                        in_=ost[:, t % 2, :]).then_inc(qc.h, 16)
            qc.v += 16
    blk.scalar(p12_st)
    OUT_DONE = qc.v

    def p12_fin(e):
        e.wait_ge(qc.h, OUT_DONE)
        if dbg and "late" in dbg:
            for j in range(FC):
                e.dma_start(out=dbg_t["dbg_y7"][j * 128:(j + 1) * 128, :],
                            in_=yf[:, j, :]).then_inc(qs.h, 16)
                qs.v += 16
            for t in range(TC):
                e.dma_start(out=dbg_t["dbg_s"][:, t, :],
                            in_=stok[:, t, :]).then_inc(qs.h, 16)
                e.dma_start(out=dbg_t["dbg_g"][:, t, :],
                            in_=gtok[:, t, :]).then_inc(qs.h, 16)
                qs.v += 32
            e.wait_ge(qs.h, qs.v)
        if dbg and "fin" in dbg:
            for t in range(TC):
                e.dma_start(out=dbg_t["dbg_s"][:, t, :],
                            in_=stok[:, t, :]).then_inc(qs.h, 16)
                e.dma_start(out=dbg_t["dbg_g"][:, t, :],
                            in_=gtok[:, t, :]).then_inc(qs.h, 16)
                qs.v += 32
            for i2 in range(IC):
                e.dma_start(out=dbg_t["dbg_o"][i2 * 128:(i2 + 1) * 128, :],
                            in_=o_all[:, i2, :]).then_inc(qs.h, 16)
                qs.v += 16
            e.dma_start(out=dbg_t["dbg_sr"][:, 8:10], in_=srD_sb[:]).then_inc(qs.h, 16)
            qs.v += 16
            e.wait_ge(qs.h, qs.v)
    blk.sync(p12_fin)

    blk_ctx.__exit__(None, None, None)
    nc._ledger = {"qs": qs.v, "qc": qc.v, "qv": qv.v, "qg": qg.v, "cc": cc.v,
                  "te": te.v, "ve": ve.v, "se": se.v, "barS": barS.v}
    return nc


_CACHE = {}


def _get_program(NC, T):
    key = (NC, T)
    if key not in _CACHE:
        _CACHE[key] = build_program(NC, T)
    return _CACHE[key]


def make_inputs(core, NC, T, x, Wq, Wk, Wv, Wg, Wo, gamma, beta, w_gate,
                w_up, w_down):
    b, h = core // 2, core % 2
    xs = np.ascontiguousarray(x[b, h * T:(h + 1) * T, :], dtype=np.float32)
    USH = DM // NC
    DSH = IM // NC
    NCH = T // CH
    sel4 = np.zeros((4, 128), np.float32)
    for j in range(4):
        sel4[j, j * 32:(j + 1) * 32] = 1.0
    trl = np.zeros((128, CH), np.float32)
    for p in range(128):
        trl[p, p % 64:] = 1.0
    trl = np.tile(trl, (1, FC))
    scanm = np.ones((128, T), np.float32)
    scanm[:, ::CH] = 0.0
    selh = np.zeros((128, NC), np.float32)
    if h == 1:
        selh[:, core - 1] = 1.0
    return {
        "x_in": xs,
        "wq_t": np.ascontiguousarray(Wq.T, np.float32),
        "wk_t": np.ascontiguousarray(Wk.T, np.float32),
        "wv_t": np.ascontiguousarray(Wv.T, np.float32),
        "wg_t": np.ascontiguousarray(Wg.T, np.float32),
        "wo_t": np.ascontiguousarray(Wo.T, np.float32),
        "gamma_t": np.ascontiguousarray(gamma.reshape(FC, 128).T, np.float32),
        "beta_t": np.ascontiguousarray(beta.reshape(FC, 128).T, np.float32),
        "wup_s": np.ascontiguousarray(w_up[:, core * USH:(core + 1) * USH],
                                      np.float32),
        "wgt_s": np.ascontiguousarray(w_gate[:, core * USH:(core + 1) * USH],
                                      np.float32),
        "wdn_s": np.ascontiguousarray(w_down[:, core * DSH:(core + 1) * DSH],
                                      np.float32),
        "wup_ts": np.ascontiguousarray(w_up.T[core * USH:(core + 1) * USH, :],
                                       np.float32),
        "wgt_ts": np.ascontiguousarray(
            w_gate.T[core * USH:(core + 1) * USH, :], np.float32),
        "wdn_ts": np.ascontiguousarray(
            w_down.T[core * DSH:(core + 1) * DSH, :], np.float32),
        "eye_f": np.eye(128, dtype=np.float32),
        "sel4": sel4,
        "tril": trl.astype(bfloat16),
        "scanm": scanm,
        "onesch": np.ones((128, NCH), np.float32),
        "ones_f_1x": np.ones((1, 128), np.float32),
        "ones_bf_1x": np.ones((1, 128), bfloat16),
        "ones_bf_c1": np.ones((128, 1), bfloat16),
        "selH": selh,
    }


def kernel(L0_attn_x, L0_attn_Wq, L0_attn_Wk, L0_attn_Wv, L0_attn_Wg,
           L0_attn_Wo, L0_attn_gamma, L0_attn_beta, L0_ffn_w_gate,
           L0_ffn_w_up, L0_ffn_w_down):
    NC = 8
    x = np.asarray(L0_attn_x, dtype=np.float32)
    B, S, _ = x.shape
    T = (B * S) // NC
    nc = _get_program(NC, T)
    args = [np.asarray(a, dtype=np.float32) for a in
            (L0_attn_Wq, L0_attn_Wk, L0_attn_Wv, L0_attn_Wg, L0_attn_Wo,
             L0_attn_gamma, L0_attn_beta, L0_ffn_w_gate, L0_ffn_w_up,
             L0_ffn_w_down)]
    in_maps = [make_inputs(c, NC, T, x, *args) for c in range(NC)]
    res = bass_utils.run_bass_kernel_spmd(nc, in_maps,
                                          core_ids=list(range(NC)))
    out = np.empty((B, S, DM), np.float32)
    for c in range(NC):
        b, h = c // 2, c % 2
        out[b, h * T:(h + 1) * T, :] = res.results[c]["out"]
    return out
